# version 01
# baseline (speedup 1.0000x reference)
import contextlib
import numpy as np
import ml_dtypes
import concourse.bass as bass
import concourse.mybir as mybir
from concourse.bass_utils import run_bass_kernel_spmd

F32 = mybir.dt.float32
BF16 = mybir.dt.bfloat16
AF = mybir.ActivationFunctionType
ALU = mybir.AluOpType
AX = mybir.AxisListType

NCORES = 8
B_LOC = 2
T_SEQ = 2048
NT = B_LOC * T_SEQ
D = 1024
NCH = D // 128
TT = 512
NTT = NT // TT
BF = ml_dtypes.bfloat16

ENGS = ("sq", "te", "ve", "se", "po")


class Prog:

    def __init__(self, nc):
        self.nc = nc
        self.stream = {e: [] for e in ENGS}
        self.cnt = {}
        self.waited = {e: {} for e in ENGS}
        self._bufs = {}
        self._stack = contextlib.ExitStack()
        for e in ENGS:
            self.cnt["p_" + e] = 0
            self.cnt["d_" + e] = 0

    def buf(self, name, shape, dt):
        key = ("sb", name)
        if key not in self._bufs:
            self._bufs[key] = self._stack.enter_context(
                self.nc.sbuf_tensor(name, list(shape), dt))
        return self._bufs[key]

    def psum(self, name, shape=(128, 512), dt=F32):
        key = ("ps", name)
        if key not in self._bufs:
            self._bufs[key] = self._stack.enter_context(
                self.nc.psum_tensor(name, list(shape), dt))
        return self._bufs[key]

    def op(self, eng, builder, after=(), inc=True, dma=False):
        if dma:
            after = tuple(after) + (("d_" + eng, self.cnt["d_" + eng]),)
        waits = []
        for tok in after:
            if tok is None:
                continue
            s, v = tok
            if self.waited[eng].get(s, 0) >= v:
                continue
            self.waited[eng][s] = v
            waits.append((s, v))
        tok = None
        if inc:
            sem = ("d_" if dma else "p_") + eng
            n = 16 if dma else 1
            self.cnt[sem] += n
            tok = (sem, self.cnt[sem])
            self.stream[eng].append((waits, builder, (sem, n)))
        else:
            self.stream[eng].append((waits, builder, None))
        return tok

    def wait(self, eng, after):
        self.op(eng, None, after=after, inc=False)

    def cc(self, value):
        if not hasattr(self, "_ccmap"):
            self._ccmap = {}
        if value not in self._ccmap:
            pool = self.buf("constpool", (128, 16), F32)
            idx = len(self._ccmap)
            assert idx < 16
            ap = pool[:][:, idx:idx + 1]
            tok = self.op("ve", lambda e: e.memset(ap, float(value)))
            self._ccmap[value] = (ap, tok)
        return self._ccmap[value]

    def token(self, eng, dma=False):
        sem = ("d_" if dma else "p_") + eng
        return (sem, self.cnt[sem])

    def barrier(self, engs=ENGS):
        toks = [self.token(e) for e in engs] + [self.token(e, dma=True) for e in engs]
        toks = [t for t in toks if t[1] > 0]
        for e in engs:
            self.wait(e, toks)

    def realize(self):
        nc = self.nc
        used = set()
        for e in ENGS:
            for (waits, builder, inc) in self.stream[e]:
                for (s, v) in waits:
                    used.add(s)
                if inc is not None:
                    used.add(inc[0])
        semctx = {}
        for name in sorted(used):
            semctx[name] = self._stack.enter_context(nc.semaphore(name))
        with nc.Block() as block:
            decos = {"sq": block.sync, "te": block.tensor, "ve": block.vector,
                     "se": block.scalar, "po": block.gpsimd}
            for e in ENGS:
                def body(ep, stream_e=self.stream[e]):
                    for (waits, builder, inc) in stream_e:
                        for (s, v) in waits:
                            ep.wait_ge(semctx[s], v)
                        if builder is not None:
                            inst = builder(ep)
                            if inc is not None:
                                inst.then_inc(semctx[inc[0]], inc[1])
                decos[e](body)
        self._stack.close()



def r_cm(ap):
    return ap.rearrange("(a p) t -> p a t", p=128)


def emit_const_init(P):
    oc = P.buf("ones_col", (128, 1), F32)
    orow = P.buf("ones_row", (1, 128), F32)
    t1 = P.op("ve", lambda e: e.memset(oc[:], 1.0))
    t2 = P.op("ve", lambda e: e.memset(orow[:], 1.0))
    return (t1, t2)


def load_w_bf16(P, name, src_ap, m0=0, m1=None, after=()):
    cin, M = src_ap.shape
    if m1 is None:
        m1 = M
    mm = m1 - m0
    nk = cin // 128
    dst = P.buf(name, (128, nk, mm), BF16)
    stage = P.buf("wstage", (128, 4608), F32)
    src_r = src_ap.rearrange("(a p) m -> p a m", p=128)
    tv_prev = None
    for k in range(nk):
        td = P.op("sq", lambda e, k=k: e.dma_start(out=stage[:][:, :mm], in_=src_r[:, k, m0:m1]),
                  after=tuple(after) + ((tv_prev,) if tv_prev else ()), dma=True)
        tv_prev = P.op("ve", lambda e, k=k: e.tensor_copy(dst[:][:, k, :], stage[:][:, :mm]),
                       after=(td,))
    return dst, tv_prev


def emit_ln_tile(P, xin_r, j, xn_dst, ps_a, ps_b, after=(), gb=None, xf_name="ln_xf", loader=None):
    xf = P.buf(xf_name, (128, NCH, TT), F32)
    sq = P.buf("ln_sq", (128, NCH, TT), F32)
    rows = getattr(P, "ln_rows_alias", None)
    if rows is None:
        rows = P.buf("ln_r2", (128, 3 * TT), F32)[:]
    tmp = getattr(P, "ln_tmp_alias", None)
    if tmp is None:
        tmp = P.buf("ln_tmp", (128, TT), F32)[:]
    oc = P.buf("ones_col", (128, 1), F32)
    orow = P.buf("ones_row", (1, 128), F32)
    meanr = rows[0:1, 0:TT]
    ex2 = rows[0:1, TT:2 * TT]
    m2 = rows[0:1, 2 * TT:3 * TT]
    rstd = m2
    mrs = meanr

    if loader is None:
        t_x = P.op("sq", lambda e: e.dma_start(out=xf[:], in_=xin_r[:, :, j * TT:(j + 1) * TT]),
                   after=after, dma=True)
    else:
        t_x = loader(xf, after)
    t_sq = P.op("se", lambda e: e.activation(sq[:].rearrange("p a t -> p (a t)"),
                                             xf[:].rearrange("p a t -> p (a t)"), AF.Square),
                after=(t_x,))
    for k in range(NCH):
        P.op("te", lambda e, k=k: e.matmul(ps_a[:][0:1, :TT], oc[:], xf[:][:, k, :],
                                           start=(k == 0), stop=(k == NCH - 1)),
             after=(t_x,), inc=(k == NCH - 1))
    t_s1 = P.token("te")
    for k in range(NCH):
        P.op("te", lambda e, k=k: e.matmul(ps_b[:][0:1, :TT], oc[:], sq[:][:, k, :],
                                           start=(k == 0), stop=(k == NCH - 1)),
             after=(t_sq,), inc=(k == NCH - 1))
    t_s2 = P.token("te")
    t1 = P.op("ve", lambda e: e.tensor_scalar(meanr, ps_a[:][0:1, :TT], 1.0 / D, None, ALU.mult),
              after=(t_s1,))
    t2 = P.op("ve", lambda e: e.tensor_scalar(ex2, ps_b[:][0:1, :TT], 1.0 / D, None, ALU.mult),
              after=(t_s2, t1))
    t3 = P.op("ve", lambda e: e.tensor_tensor(m2, meanr, meanr, ALU.mult), after=(t2,))
    t4 = P.op("ve", lambda e: e.tensor_tensor(ex2, ex2, m2, ALU.subtract), after=(t3,))
    eps_ap, eps_tok = P.cc(1e-5)
    t5 = P.op("se", lambda e: e.activation(ex2, ex2, AF.Ln, bias=eps_ap[0:1, :]), after=(t4, eps_tok))
    t6 = P.op("se", lambda e: e.activation(rstd, ex2, AF.Exp, scale=-0.5), after=(t5,))
    t8 = P.op("ve", lambda e: e.tensor_tensor(mrs, meanr, rstd, ALU.mult), after=(t6,))
    t_ba = P.op("te", lambda e: e.matmul(ps_a[:][:, :TT], orow[:], rstd, start=True, stop=True),
                after=(t6, t1, t2))
    t_bb = P.op("te", lambda e: e.matmul(ps_b[:][:, :TT], orow[:], mrs, start=True, stop=True),
                after=(t8,))
    prev = None
    for k in range(NCH):
        ta = P.op("ve", lambda e, k=k: e.tensor_tensor(tmp, xf[:][:, k, :], ps_a[:][:, :TT], ALU.mult),
                  after=(t_ba, t_x, prev))
        if gb is None:
            prev = P.op("ve", lambda e, k=k: e.tensor_tensor(xn_dst[:, k, :], tmp, ps_b[:][:, :TT], ALU.subtract),
                        after=(t_bb, ta))
        else:
            tb = P.op("ve", lambda e, k=k: e.tensor_tensor(tmp, tmp, ps_b[:][:, :TT], ALU.subtract),
                      after=(t_bb, ta))
            g_col, b_col = gb
            prev = P.op("ve", lambda e, k=k: e.tensor_scalar(
                xn_dst[:, k, :], tmp, g_col[:, k:k + 1], b_col[:, k:k + 1], ALU.mult, ALU.add),
                after=(tb,))
    return prev, t_sq, xf


def emit_ln_full(P, xin_r, xn_dst, ps_a, ps_b, after=(), gb=None):
    tok = None
    t_prev_reads = tuple(after)
    for j in range(NTT):
        tok, t_free, _ = emit_ln_tile(
            P, xin_r, j, xn_dst[:, :, j * TT:(j + 1) * TT], ps_a, ps_b,
            after=t_prev_reads, gb=gb)
        t_prev_reads = (tok, t_free)
    return tok



NEXP = 8
DFF = 4096


def emit_moe_conv(P, w, tok0):
    cs_f = P.buf("ln_xf", (128, NCH, TT), F32)[:].rearrange("p a t -> p (a t)")
    cs_b = P.buf("ln_sq", (128, NCH, TT), F32)[:].rearrange("p a t -> p (a t)").bitcast(BF16)[:, 0:4096]
    t_in_prev, t_cp_prev, t_out_prev = None, None, None
    for e in range(NEXP):
        w1e = w["W1"][e, :, :].rearrange("(a p) f -> p a f", p=128)
        w2e = w["W2"][e, :, :].rearrange("(a p) f -> p a f", p=128)

        for a in range(NCH):
            t_in = P.op("sq", lambda ep, w1e=w1e, a=a: ep.dma_start(out=cs_f, in_=w1e[:, a, :]),
                        after=(tok0, t_cp_prev), dma=True)
            t_cp = P.op("ve", lambda ep: ep.tensor_copy(cs_b, cs_f), after=(t_in, t_out_prev))
            t_out = P.op("po", lambda ep, e=e, a=a: ep.dma_start(
                out=w["W1b"].rearrange("(a p) f -> p a f", p=128)[:, e * NCH + a, :], in_=cs_b[:]),
                after=(t_cp,), dma=True)
            t_cp_prev, t_out_prev = t_cp, t_out
        for b in range(NCH):
            t_in = P.op("sq", (lambda ep, w2e=w2e, b=b: ep.dma_start(
                out=cs_f.rearrange("p (a f) -> p a f", f=1024), in_=w2e[:, 4 * b:4 * b + 4, :])),
                        after=(tok0, t_cp_prev), dma=True)
            t_cp = P.op("ve", lambda ep: ep.tensor_copy(cs_b, cs_f), after=(t_in, t_out_prev))
            t_out = P.op("po", lambda ep, e=e, b=b: ep.dma_start(
                out=w["W2b"].rearrange("(a p) f -> p a f", p=128)[:, e * NCH + b, :], in_=cs_b[:]),
                after=(t_cp,), dma=True)
            t_cp_prev, t_out_prev = t_cp, t_out
    return t_out_prev


def emit_moe(P, xin, xout, w, tok0, t_conv):
    P.ln_tmp_alias = None
    P.ln_rows_alias = None
    """w keys: W1b/W2b (bf16 scratch dram, [1024, 4*4096]-flat views), Wr [1024,8],
    br [1,8], b1c [8,128,32], b2 [8,1024], ident [128,128] fp32."""
    xin_r, xout_r = r_cm(xin), r_cm(xout)
    pb = [P.psum("pb%d" % i) for i in range(6)]
    TG = 1024
    NG = NT // TG
    NTG = TG // TT
    xn = P.buf("moe_xn", (128, NCH, TG), BF16)
    h1c = P.buf("moe_h1", (128, 16, TT), BF16)
    yg = P.buf("moe_y", (128, NCH, TG), F32)
    selg = P.buf("moe_sel", (128, 64), F32)
    selT = P.buf("moe_selT", (8, TG), F32)
    selb = P.buf("moe_selb", (128, TG), F32)
    ytmp = P.buf("moe_tmp", (128, TT), F32)
    lgb = P.buf("moe_lg", (128, 8), F32)
    cols = P.buf("moe_cols", (128, 4), F32)
    w1h = P.buf("moe_w1h", (128, NCH, 2048), BF16)
    w2h = P.buf("moe_w2h", (128, 16, 1024), BF16)
    wr_sb = P.buf("moe_wr", (128, NCH, 8), BF16)
    wr_f = P.buf("moe_wrf", (128, NCH, 8), F32)
    br_b = P.buf("moe_br", (128, 8), F32)
    b1_sb = P.buf("moe_b1", (128, 32), F32)
    b2_sb = P.buf("moe_b2", (8, 1024), F32)
    id_sb = P.buf("ident_f", (128, 128), F32)
    emask = P.buf("moe_emask", (8, 8 * 128), F32)
    orow = P.buf("ones_row", (1, 128), F32)
    W1b_r = w["W1b"].rearrange("(a p) f -> p a f", p=128)
    W2b_r = w["W2b"].rearrange("(a p) f -> p a f", p=128)

    t_wr = P.op("sq", lambda ep: ep.dma_start(out=wr_f[:], in_=w["Wr"].rearrange("(a p) m -> p a m", p=128)), dma=True)
    t_wr = P.op("ve", lambda ep: ep.tensor_copy(wr_sb[:], wr_f[:]), after=(t_wr,))
    t_id = P.op("sq", lambda ep: ep.dma_start(out=id_sb[:], in_=w["ident"]), dma=True)
    t_em = P.op("sq", lambda ep: ep.dma_start(out=emask[:], in_=w["emask"]), dma=True)
    t_b2 = P.op("sq", lambda ep: ep.dma_start(out=b2_sb[:], in_=w["b2"]), dma=True)
    t_brr = P.op("sq", lambda ep: ep.dma_start(out=lgb[:][0:1, :], in_=w["br"]), dma=True)
    t_brb = P.op("te", lambda ep: ep.matmul(pb[5][:][:, 0:8], orow[:], lgb[:][0:1, :],
                                            start=True, stop=True), after=(t_brr, tok0))
    t_brb = P.op("ve", lambda ep: ep.tensor_copy(br_b[:], pb[5][:][:, 0:8]), after=(t_brb,))

    t_done = []
    t_xn_cons = None
    t_prev_ln = (tok0, t_conv)
    t_yg_cons = None
    t_w1_cons, t_w2_cons = None, None
    t_h1_cons = None
    t_b1_cons = None
    t_selb_cons = None
    t_selT_cons = None
    for g in range(NG):
        t_xn = None
        t_rt = {}
        t_rt_prev = None
        for jj in range(NTG):
            t_xn, t_free, xf = emit_ln_tile(
                P, xin_r, g * NTG + jj, xn[:][:, :, jj * TT:(jj + 1) * TT],
                pb[0], pb[1], after=t_prev_ln + (t_xn_cons,))
            for st4 in range(TT // 128):
                st = jj * (TT // 128) + st4
                for k in range(NCH):
                    P.op("te", lambda ep, st=st, jj=jj, st4=st4, k=k: ep.matmul(
                        pb[4][:][:, st * 8:(st + 1) * 8],
                        xn[:][:, k, jj * TT + st4 * 128:jj * TT + (st4 + 1) * 128],
                        wr_sb[:][:, k, :],
                        start=(k == 0), stop=(k == NCH - 1)),
                        after=(t_xn, t_wr, t_selT_cons, t_rt_prev), inc=(k == NCH - 1))
                t_rt[st] = t_rt_prev = P.token("te")
            t_prev_ln = (t_xn, t_free, t_rt_prev)
        t_selT = []
        t_rt_all = t_rt_prev
        for st in range(TG // 128):
            t_l1 = P.op("ve", lambda ep, st=st: ep.tensor_tensor(
                lgb[:], pb[4][:][:, st * 8:(st + 1) * 8], br_b[:], ALU.add),
                after=(t_rt_all, t_brb))
            t_l2 = P.op("ve", lambda ep: ep.tensor_reduce(
                cols[:][:, 0:1], lgb[:], AX.X, ALU.max, negate=True), after=(t_l1,))
            t_l3 = P.op("se", lambda ep: ep.activation(
                lgb[:], lgb[:], AF.Exp, bias=cols[:][:, 0:1], accum_out=cols[:][:, 1:2]),
                after=(t_l2,))
            t_l4 = P.op("ve", lambda ep: ep.reciprocal(cols[:][:, 1:2], cols[:][:, 1:2]),
                        after=(t_l3,))
            t_l5 = P.op("ve", lambda ep: ep.tensor_scalar(
                lgb[:], lgb[:], cols[:][:, 1:2], None, ALU.mult), after=(t_l4,))
            t_l6 = P.op("ve", lambda ep: ep.tensor_reduce(
                cols[:][:, 2:3], lgb[:], AX.X, ALU.max), after=(t_l5,))
            t_l7 = P.op("ve", lambda ep, st=st: ep.tensor_scalar(
                selg[:][:, st * 8:(st + 1) * 8], lgb[:], cols[:][:, 2:3], cols[:][:, 2:3],
                ALU.is_ge, ALU.mult), after=(t_l6,))
            t_tp = P.op("te", lambda ep, st=st: ep.transpose(
                pb[5][:][0:8, 0:128], selg[:][:, st * 8:(st + 1) * 8], id_sb[:]),
                after=(t_l7, t_id, t_selT_cons) + ((t_selT[-1],) if t_selT else ()))
            t_selT.append(P.op("ve", lambda ep, st=st: ep.tensor_copy(
                selT[:][:, st * 128:(st + 1) * 128], pb[5][:][0:8, 0:128]), after=(t_tp,)))
        t_selT = tuple(t_selT)
        t_yg = []
        t_b2prev = {0: None, 1: None}
        for hm in range(NCH):
            for tt in range(NTG):
                t_b2m = P.op("te", lambda ep, hm=hm, tt=tt: ep.matmul(
                    pb[2 + tt][:][:, :TT], b2_sb[:][:, hm * 128:(hm + 1) * 128],
                    selT[:][:, tt * TT:(tt + 1) * TT], start=True, stop=True),
                    after=t_selT + (t_b2, t_yg_cons, t_b2prev[tt]))
                t_b2prev[tt] = P.op("ve", lambda ep, hm=hm, tt=tt: ep.tensor_copy(
                    yg[:][:, hm, tt * TT:(tt + 1) * TT], pb[2 + tt][:][:, :TT]),
                    after=(t_b2m,))
                t_yg.append(t_b2prev[tt])
        t_yg = tuple(t_yg)
        t_ypart_prev = {0: t_yg[-1], 1: t_yg[-1]}
        for e2 in range(NEXP):
            t_sb = []
            for tt in range(NTG):
                t_k1m = P.op("te", lambda ep, e2=e2, tt=tt: ep.matmul(
                    pb[5][:][:, :TT], emask[:][:, e2 * 128:(e2 + 1) * 128],
                    selT[:][:, tt * TT:(tt + 1) * TT],
                    start=True, stop=True),
                    after=t_selT + (t_selb_cons, t_em, t_ypart_prev[0], t_ypart_prev[1])
                    + ((t_sb[-1],) if t_sb else ()))
                t_sb.append(P.op("ve", lambda ep, tt=tt: ep.tensor_copy(
                    selb[:][:, tt * TT:(tt + 1) * TT], pb[5][:][:, :TT]), after=(t_k1m,)))
            t_sb = tuple(t_sb)
            t_b1 = P.op("sq", lambda ep, e2=e2: ep.dma_start(
                out=b1_sb[:], in_=w["b1c"][e2, :, :]), after=(t_h1_cons, t_b1_cons), dma=True)
            for fc in range(2):
                t_w1h = P.op("po", lambda ep, e2=e2, fc=fc: ep.dma_start(
                    out=w1h[:], in_=W1b_r[:, e2 * NCH:(e2 + 1) * NCH, fc * 2048:(fc + 1) * 2048]),
                    after=(t_conv, t_w1_cons), dma=True)
                t_w2h = P.op("po", lambda ep, e2=e2, fc=fc: ep.dma_start(
                    out=w2h[:].rearrange("p a f -> p (a f)").rearrange("p (b x) -> p b x", x=4096),
                    in_=W2b_r[:, e2 * NCH + fc * 4:e2 * NCH + (fc + 1) * 4, :]),
                    after=(t_conv, t_w2_cons), dma=True)
                for tt in range(NTG):
                    t_h1 = []
                    for fm in range(16):
                        psz = pb[fm % 2]
                        for k in range(NCH):
                            P.op("te", lambda ep, fm=fm, tt=tt, k=k, psz=psz: ep.matmul(
                                psz[:][:, :TT], w1h[:][:, k, fm * 128:(fm + 1) * 128],
                                xn[:][:, k, tt * TT:(tt + 1) * TT],
                                start=(k == 0), stop=(k == NCH - 1)),
                                after=(t_w1h, t_xn, t_h1[-2] if len(t_h1) >= 2 else t_h1_cons),
                                inc=(k == NCH - 1))
                        t_zmm = P.token("te")
                        t_h1.append(P.op("se", lambda ep, fm=fm, fc=fc, psz=psz: ep.activation(
                            h1c[:][:, fm, :], psz[:][:, :TT], AF.Relu,
                            bias=b1_sb[:][:, fc * 16 + fm:fc * 16 + fm + 1]),
                            after=(t_zmm, t_b1, t_h1_cons)))
                    t_h1 = tuple(t_h1)
                    t_b1_cons = t_h1[-1]
                    for hm in range(NCH):
                        psy = pb[2 + hm % 2]
                        for k2 in range(16):
                            P.op("te", lambda ep, hm=hm, k2=k2, psy=psy: ep.matmul(
                                psy[:][:, :TT], w2h[:][:, k2, hm * 128:(hm + 1) * 128],
                                h1c[:][:, k2, :], start=(k2 == 0), stop=(k2 == 15)),
                                after=(t_w2h,) + t_h1 + (t_ypart_prev[hm % 2],),
                                inc=(k2 == 15))
                        t_ymm = P.token("te")
                        t_y1 = P.op("ve", lambda ep, tt=tt, psy=psy: ep.tensor_tensor(
                            ytmp[:], psy[:][:, :TT],
                            selb[:][:, tt * TT:(tt + 1) * TT], ALU.mult),
                            after=(t_ymm,) + t_sb)
                        t_y2 = P.op("ve", lambda ep, hm=hm, tt=tt: ep.tensor_tensor(
                            yg[:][:, hm, tt * TT:(tt + 1) * TT],
                            yg[:][:, hm, tt * TT:(tt + 1) * TT],
                            ytmp[:], ALU.add),
                            after=(t_y1,) + t_yg)
                        t_ypart_prev[hm % 2] = t_y2
                    t_h1_cons = P.token("te")
                t_w1_cons = P.token("te")
                t_w2_cons = P.token("te")
            t_selb_cons = P.token("te")
        t_yg_last = (t_ypart_prev[0], t_ypart_prev[1])
        xt = P.buf("moe_xt", (128, TT), F32)
        t_xt_cons = None
        for a in range(NCH):
            for tt in range(NTG):
                jt = g * NTG + tt
                t_xl = P.op("sq", lambda ep, a=a, jt=jt: ep.dma_start(
                    out=xt[:], in_=xin_r[:, a, jt * TT:(jt + 1) * TT]),
                    after=(t_xt_cons,), dma=True)
                t_add = P.op("ve", lambda ep, a=a, tt=tt: ep.tensor_tensor(
                    xt[:], xt[:], yg[:][:, a, tt * TT:(tt + 1) * TT], ALU.add),
                    after=(t_xl,) + t_yg_last)
                t_st = P.op("se", lambda ep, a=a, jt=jt: ep.dma_start(
                    out=xout_r[:, a, jt * TT:(jt + 1) * TT], in_=xt[:]),
                    after=(t_add,), dma=True)
                t_xt_cons = t_st
                t_done.append(t_st)
        t_yg_cons = t_xt_cons
        t_xn_cons = P.token("te")
        t_selT_cons = P.token("te")
    return tuple(t_done)


def build_s6():
    nc = bass.Bass("TRN2", target_bir_lowering=False)
    xin = nc.dram_tensor("x_in", [D, NT], F32, kind="ExternalInput").ap()
    w = {}
    w["W1"] = nc.dram_tensor("W1", [NEXP, D, DFF], F32, kind="ExternalInput").ap()
    w["W2"] = nc.dram_tensor("W2", [NEXP, DFF, D], F32, kind="ExternalInput").ap()
    w["Wr"] = nc.dram_tensor("Wr", [D, NEXP], F32, kind="ExternalInput").ap()
    w["br"] = nc.dram_tensor("br", [1, NEXP], F32, kind="ExternalInput").ap()
    w["b1c"] = nc.dram_tensor("b1c", [NEXP, 128, 32], F32, kind="ExternalInput").ap()
    w["b2"] = nc.dram_tensor("b2", [NEXP, D], F32, kind="ExternalInput").ap()
    w["ident"] = nc.dram_tensor("ident", [128, 128], F32, kind="ExternalInput").ap()
    w["emask"] = nc.dram_tensor("emask", [NEXP, NEXP * 128], F32, kind="ExternalInput").ap()
    w["W1b"] = nc.dram_tensor("W1b", [128 * 64, DFF], BF16, kind="Internal").ap()
    w["W2b"] = nc.dram_tensor("W2b", [128 * 64, DFF], BF16, kind="Internal").ap()
    xout = nc.dram_tensor("x_out", [D, NT], F32, kind="ExternalOutput").ap()
    P = Prog(nc)
    tok0 = emit_const_init(P)[-1]
    t_conv = emit_moe_conv(P, w, tok0)
    toks = emit_moe(P, xin, xout, w, tok0, t_conv)
    _finish(P, toks)
    P.realize()
    return nc





def emit_conv_w(P, src_ap, dst_ap, tok0, state):
    cin, M = src_ap.shape
    nk = (cin + 127) // 128
    cs_f = P.buf("ln_xf", (128, NCH, TT), F32)[:].rearrange("p a t -> p (a t)")
    cs_b = P.buf("ln_sq", (128, NCH, TT), F32)[:].rearrange("p a t -> p (a t)").bitcast(BF16)[:, 0:4096]
    for k in range(nk):
        rows = min(128, cin - k * 128)
        for m0 in range(0, M, 4096):
            mm = min(4096, M - m0)
            t_in = P.op("sq", lambda ep, k=k, m0=m0, mm=mm, rows=rows: ep.dma_start(
                out=cs_f[0:rows, :mm], in_=src_ap[k * 128:k * 128 + rows, m0:m0 + mm]),
                after=(tok0, state.get("cp")), dma=True)
            t_cp = P.op("ve", lambda ep, mm=mm, rows=rows: ep.tensor_copy(
                cs_b[0:rows, :mm], cs_f[0:rows, :mm]),
                after=(t_in, state.get("out")))
            t_out = P.op("po", lambda ep, k=k, m0=m0, mm=mm, rows=rows: ep.dma_start(
                out=dst_ap[k * 128:k * 128 + rows, m0:m0 + mm], in_=cs_b[0:rows, :mm]),
                after=(t_cp,), dma=True)
            state["cp"], state["out"] = t_cp, t_out
    return state["out"]



def emit_s3(P, xin, xout, w, tok0, t_conv):
    P.ln_tmp_alias = None
    P.ln_rows_alias = None
    """w: Wqkvb [1024, 3072] bf16 scratch, Woutb [1024, 1024] bf16 scratch,
    ident_bf [128,128] bf16, ropec/ropes [128, 16, 64] fp32 tables."""
    xin_r, xout_r = r_cm(xin), r_cm(xout)
    pb = [P.psum("pb%d" % i) for i in range(6)]
    pbt = P.psum("pbt", (128, 512), BF16)
    pbt2 = P.psum("pbt2", (128, 512), BF16)
    NH, HD = 16, 64
    TS2 = T_SEQ
    NTS = TS2 // 128
    xn_s = P.buf("s3_xn", (128, NCH, TS2), BF16)
    atT = P.buf("s3_atT", (128, NCH, TS2), BF16)
    wkv_sb = P.buf("s3_w", (128, NCH, 2048), BF16)
    ftile = P.buf("s3_ft", (128, 1024), BF16)
    vtile = P.buf("s3_vt", (128, 1024), BF16)
    kvs = P.buf("s3_kvs", (64, NH * 65), F32)
    kvh = P.buf("s3_kvh", (64, NH * 65), BF16)
    qfT = P.buf("s3_qfT", (64, 128), BF16)
    tmpf = P.buf("s3_tmp", (128, 1024), F32)
    tmpg = P.buf("s3_tmp2", (128, 1024), F32)
    dcol = P.buf("s3_dcol", (128, 2), F32)
    ropec = P.buf("ropec_sb", (128, NTS, HD), F32)
    ropes = P.buf("ropes_sb", (128, NTS, HD), F32)
    id_bf = P.buf("idbf_sb", (128, 128), BF16)
    oc_bf = P.buf("ones_col_bf", (128, 1), BF16)
    xtile = [P.buf("s3_xt0", (128, TT), F32), P.buf("s3_xt1", (128, TT), F32)]

    t_rc = P.op("sq", lambda ep: ep.dma_start(out=ropec[:], in_=w["ropec"]), dma=True)
    t_rc = P.op("sq", lambda ep: ep.dma_start(out=ropes[:], in_=w["ropes"]), dma=True)
    t_rc = P.op("sq", lambda ep: ep.dma_start(out=id_bf[:], in_=w["ident_bf"]), dma=True)
    t_ob = P.op("ve", lambda ep: ep.memset(oc_bf[:], 1.0), after=(tok0,))

    Wq_r = w["Wqkvb"].rearrange("(a p) m -> p a m", p=128)
    Wo_r = w["Woutb"].rearrange("(a p) m -> p a m", p=128)
    t_done = []
    prev_ln = (tok0, t_conv)
    t_w_cons = None
    t_xt_cons = {0: None, 1: None}
    t_wo_cons = {0: None, 1: None}
    woslb = [P.buf("s3_wo0", (128, NCH, 128), BF16), P.buf("s3_wo1", (128, NCH, 128), BF16)]

    def rope_elu(psx0, psx1, dst, scale_mul, t16, after):
        cosb = ropec[:][:, t16, :].rearrange("p d -> p () d").broadcast_to([128, 8, HD])
        sinb = ropes[:][:, t16, :].rearrange("p d -> p () d").broadcast_to([128, 8, HD])
        toks = []
        for half, psx in ((0, psx0), (1, psx1)):
            v3 = psx[:].rearrange("p (h d) -> p h d", d=HD)
            tf3 = tmpf[:][:, half * 512:(half + 1) * 512].rearrange("p (h d) -> p h d", d=HD)
            tg3 = tmpg[:][:, half * 512:(half + 1) * 512].rearrange("p (h d) -> p h d", d=HD)
            t1 = P.op("ve", lambda ep, v3=v3, tf3=tf3: ep.tensor_tensor(tf3, v3, cosb, ALU.mult),
                      after=after)
            t2 = P.op("ve", lambda ep, v3=v3, tg3=tg3: ep.tensor_copy(tg3[:, :, 0:32], v3[:, :, 32:64]),
                      after=after)
            t3 = P.op("ve", lambda ep, v3=v3, tg3=tg3: ep.tensor_copy(tg3[:, :, 32:64], v3[:, :, 0:32]),
                      after=after + (t2,))
            t4 = P.op("ve", lambda ep, tg3=tg3: ep.tensor_tensor(tg3, tg3, sinb, ALU.mult),
                      after=(t2, t3))
            toks.append((t1, t4))
        t5 = P.op("ve", lambda ep: ep.tensor_tensor(tmpf[:], tmpf[:], tmpg[:], ALU.add),
                  after=toks[0] + toks[1])
        t6 = P.op("ve", lambda ep: ep.tensor_scalar(tmpg[:], tmpf[:], scale_mul, 0.0, ALU.mult, ALU.min),
                  after=(t5,))
        t7 = P.op("se", lambda ep: ep.activation(tmpg[:], tmpg[:], AF.Exp), after=(t6,))
        t8 = P.op("ve", lambda ep: ep.tensor_scalar(tmpf[:], tmpf[:], scale_mul, 0.0, ALU.mult, ALU.max),
                  after=(t5, t7))
        t9 = P.op("ve", lambda ep, dst=dst: ep.tensor_tensor(dst, tmpf[:], tmpg[:], ALU.add),
                  after=(t8, t7))
        return t9

    for s in range(B_LOC):
        t_xn = None
        for jj in range(TS2 // TT):
            t_xn, t_free, _ = emit_ln_tile(
                P, xin_r, s * (TS2 // TT) + jj, xn_s[:][:, :, jj * TT:(jj + 1) * TT],
                pb[0], pb[1], after=prev_ln)
            prev_ln = (t_xn, t_free)
        t_wl = P.op("po", lambda ep: ep.dma_start(out=wkv_sb[:], in_=Wq_r[:, :, 1024:3072]),
                    after=(t_conv, t_w_cons), dma=True)
        t_kv_add = None
        t_ft_cons = None
        t_vt_cons = None
        for t16 in range(NTS):
            for half in range(2):
                for k in range(NCH):
                    P.op("te", lambda ep, t16=t16, half=half, k=k, s=s: ep.matmul(
                        pb[half][:][:, :TT],
                        xn_s[:][:, k, t16 * 128:(t16 + 1) * 128],
                        wkv_sb[:][:, k, half * 512:(half + 1) * 512],
                        start=(k == 0), stop=(k == NCH - 1)),
                        after=(t_xn, t_wl, t_ft_cons), inc=(k == NCH - 1))
                for k in range(NCH):
                    P.op("te", lambda ep, t16=t16, half=half, k=k: ep.matmul(
                        pb[2 + half][:][:, :TT],
                        xn_s[:][:, k, t16 * 128:(t16 + 1) * 128],
                        wkv_sb[:][:, k, 1024 + half * 512:1024 + (half + 1) * 512],
                        start=(k == 0), stop=(k == NCH - 1)),
                        after=(t_xn, t_wl, t_vt_cons), inc=(k == NCH - 1))
            t_kmm = P.token("te")
            t_kf = rope_elu(pb[0], pb[1], ftile[:], 1.0, t16, (t_kmm,))
            t_ft_cons = t_kf
            t_v0 = P.op("ve", lambda ep: ep.tensor_copy(vtile[:][:, 0:512], pb[2][:][:, :TT]),
                        after=(t_kmm,))
            t_v1 = P.op("ve", lambda ep: ep.tensor_copy(vtile[:][:, 512:1024], pb[3][:][:, :TT]),
                        after=(t_kmm,))
            t_vt_cons = t_v1
            for h in range(NH):
                psk = [pb[4], pb[5]][h % 2]
                t_m1 = P.op("te", lambda ep, h=h, psk=psk: ep.matmul(
                    psk[:][0:64, 0:64], ftile[:][:, h * 64:(h + 1) * 64],
                    vtile[:][:, h * 64:(h + 1) * 64], start=True, stop=True),
                    after=(t_kf, t_v0, t_v1, t_kv_add), inc=False)
                t_m2 = P.op("te", lambda ep, h=h, psk=psk: ep.matmul(
                    psk[:][0:64, 64:65], ftile[:][:, h * 64:(h + 1) * 64],
                    oc_bf[:], start=False, stop=True),
                    after=(t_ob,))
                if t16 == 0:
                    t_kv_add = P.op("ve", lambda ep, h=h, psk=psk: ep.tensor_copy(
                        kvs[:][:, h * 65:(h + 1) * 65], psk[:][0:64, 0:65]), after=(t_m2,))
                else:
                    t_kv_add = P.op("ve", lambda ep, h=h, psk=psk: ep.tensor_tensor(
                        kvs[:][:, h * 65:(h + 1) * 65], kvs[:][:, h * 65:(h + 1) * 65],
                        psk[:][0:64, 0:65], ALU.add), after=(t_m2,))
        t_kvh = P.op("ve", lambda ep: ep.tensor_copy(kvh[:], kvs[:]), after=(t_kv_add,))
        t_wl2 = P.op("po", lambda ep: ep.dma_start(
            out=wkv_sb[:][:, :, 0:1024], in_=Wq_r[:, :, 0:1024]),
            after=(P.token("te"),), dma=True)
        t_at_tp = None
        t_qfT_cons = None
        for t16 in range(NTS):
            for half in range(2):
                for k in range(NCH):
                    P.op("te", lambda ep, t16=t16, half=half, k=k: ep.matmul(
                        pb[half][:][:, :TT],
                        xn_s[:][:, k, t16 * 128:(t16 + 1) * 128],
                        wkv_sb[:][:, k, half * 512:(half + 1) * 512],
                        start=(k == 0), stop=(k == NCH - 1)),
                        after=(t_xn, t_wl2, t_ft_cons), inc=(k == NCH - 1))
            t_qmm = P.token("te")
            t_qf = rope_elu(pb[0], pb[1], ftile[:], 0.125, t16, (t_qmm,))
            t_ft_cons = t_qf
            t_att = []
            for h in range(NH):
                pst = [pbt, pbt2][h % 2]
                t_tp = P.op("te", lambda ep, h=h, pst=pst: ep.transpose(
                    pst[:][0:64, 0:128], ftile[:][:, h * 64:(h + 1) * 64], id_bf[:]),
                    after=(t_qf, t_qfT_cons, t_at_tp))
                t_qc = P.op("ve", lambda ep, pst=pst: ep.tensor_copy(qfT[:], pst[:][0:64, 0:128]),
                            after=(t_tp,))
                t_nd = P.op("te", lambda ep, h=h: ep.matmul(
                    pb[4][:][:, 0:65], qfT[:], kvh[:][:, h * 65:(h + 1) * 65],
                    start=True, stop=True), after=(t_qc, t_kvh, t_att[-1] if t_att else None))
                t_qfT_cons = t_nd
                t_d1 = P.op("ve", lambda ep: ep.tensor_scalar(
                    dcol[:][:, 0:1], pb[4][:][:, 64:65], 1e-6, None, ALU.max), after=(t_nd,))
                t_d2 = P.op("ve", lambda ep: ep.reciprocal(dcol[:][:, 0:1], dcol[:][:, 0:1]),
                            after=(t_d1,))
                t_at = P.op("ve", lambda ep, h=h: ep.tensor_scalar(
                    tmpg[:][:, h * 64:(h + 1) * 64], pb[4][:][:, 0:64], dcol[:][:, 0:1],
                    None, ALU.mult), after=(t_d2,))
                t_att.append(t_at)
            t_cast = P.op("ve", lambda ep: ep.tensor_copy(ftile[:], tmpg[:]), after=tuple(t_att))
            t_ft_cons = t_cast
            for a in range(NCH):
                pst = [pbt, pbt2][a % 2]
                t_tp = P.op("te", lambda ep, a=a, pst=pst: ep.transpose(
                    pst[:][:, 0:128], ftile[:][:, a * 128:(a + 1) * 128], id_bf[:]),
                    after=(t_cast, t_at_tp))
                t_at_tp = P.op("ve", lambda ep, a=a, t16=t16, pst=pst: ep.tensor_copy(
                    atT[:][:, a, t16 * 128:(t16 + 1) * 128], pst[:][:, 0:128]), after=(t_tp,))
        t_at_all = (t_at_tp,)
        for j2 in range(NCH):
            bi = j2 % 2
            t_w = P.op("po", lambda ep, j2=j2, bi=bi: ep.dma_start(
                out=woslb[bi][:], in_=Wo_r[:, :, j2 * 128:(j2 + 1) * 128]),
                after=(t_wo_cons[bi], t_conv), dma=True)
            for jj in range(TS2 // TT):
                jt = s * (TS2 // TT) + jj
                psx = [pb[2], pb[3]][jj % 2]
                xbi = jj % 2
                t_xl = P.op("sq", lambda ep, j2=j2, jt=jt, xbi=xbi: ep.dma_start(
                    out=xtile[xbi][:], in_=xin_r[:, j2, jt * TT:(jt + 1) * TT]),
                    after=(t_xt_cons[xbi],), dma=True)
                for m in range(NCH):
                    P.op("te", lambda ep, jj=jj, m=m, bi=bi, psx=psx: ep.matmul(
                        psx[:][:, :TT], woslb[bi][:][:, m, :],
                        atT[:][:, m, jj * TT:(jj + 1) * TT],
                        start=(m == 0), stop=(m == NCH - 1)),
                        after=(t_w, t_xt_cons[xbi]) + t_at_all,
                        inc=(m == NCH - 1))
                t_mm = P.token("te")
                t_add = P.op("ve", lambda ep, xbi=xbi, psx=psx: ep.tensor_tensor(
                    xtile[xbi][:], xtile[xbi][:], psx[:][:, :TT], ALU.add),
                    after=(t_mm, t_xl))
                t_st = P.op("se", lambda ep, j2=j2, jt=jt, xbi=xbi: ep.dma_start(
                    out=xout_r[:, j2, jt * TT:(jt + 1) * TT], in_=xtile[xbi][:]),
                    after=(t_add,), dma=True)
                t_xt_cons[xbi] = t_st
                t_done.append(t_st)
            t_wo_cons[bi] = P.token("te")
        prev_ln = (P.token("te"), P.token("ve"), P.token("se"))
        t_w_cons = P.token("te")
    return tuple(t_done)


def build_s3():
    nc = bass.Bass("TRN2", target_bir_lowering=False)
    xin = nc.dram_tensor("x_in", [D, NT], F32, kind="ExternalInput").ap()
    w = {}
    w["Wqkv"] = nc.dram_tensor("Wqkv", [D, 3 * D], F32, kind="ExternalInput").ap()
    w["Wout"] = nc.dram_tensor("Wout", [D, D], F32, kind="ExternalInput").ap()
    w["ident_bf"] = nc.dram_tensor("ident_bf", [128, 128], BF16, kind="ExternalInput").ap()
    w["ropec"] = nc.dram_tensor("ropec", [128, 16, 64], F32, kind="ExternalInput").ap()
    w["ropes"] = nc.dram_tensor("ropes", [128, 16, 64], F32, kind="ExternalInput").ap()
    w["Wqkvb"] = nc.dram_tensor("Wqkvb", [D, 3 * D], BF16, kind="Internal").ap()
    w["Woutb"] = nc.dram_tensor("Woutb", [D, D], BF16, kind="Internal").ap()
    xout = nc.dram_tensor("x_out", [D, NT], F32, kind="ExternalOutput").ap()
    P = Prog(nc)
    tok0 = emit_const_init(P)[-1]
    st = {}
    emit_conv_w(P, w["Wqkv"], w["Wqkvb"], tok0, st)
    t_conv = emit_conv_w(P, w["Wout"], w["Woutb"], tok0, st)
    toks = emit_s3(P, xin, xout, w, tok0, t_conv)
    _finish(P, toks)
    P.realize()
    return nc



def emit_s5(P, xin, xout, w, tok0, t_conv):
    xin_r, xout_r = r_cm(xin), r_cm(xout)
    pb = [P.psum("pb%d" % i) for i in range(6)]
    pbt = P.psum("pbt", (128, 512), BF16)
    pbt2 = P.psum("pbt2", (128, 512), BF16)
    NH, HD = 16, 64
    TS2 = T_SEQ
    NST = TS2 // 128
    NJJ = TS2 // TT
    xno = P.buf("s5_xno", (128, NCH, TS2), BF16)
    qT = P.buf("s5_qT", (128, NCH, TS2), BF16)
    kT = P.buf("s5_kT", (128, NCH, TS2), BF16)
    vtm = P.buf("s5_v", (128, NST, 1024), BF16)
    ssb = P.buf("s5_ssb", (128, TS2), F32)
    psb = P.buf("s5_psb", (128, TS2), BF16)
    bias_sb = P.buf("s5_bias", (128, TS2), F32)
    P.ln_tmp_alias = bias_sb[:][:, 0:TT]
    P.ln_rows_alias = bias_sb[:][:, TT:4 * TT]

    scols = P.buf("s5_scols", (128, 8), F32)
    ftm = P.buf("s5_ftm", (128, 16), F32)
    ftm2 = P.buf("s5_ftm2", (128, 16), F32)
    bfb = P.buf("s5_bfb", (128, 16), F32)
    _lnxf2 = P.buf("ln_xf", (128, NCH, TT), F32)[:].rearrange("p a t -> p (a t)")
    ptb = _lnxf2[:, 3104:3232].bitcast(BF16).rearrange("p (a t) -> p a t", t=128)
    otm = _lnxf2[:, 3232:3296]
    lnxf_fl = P.buf("ln_xf", (128, NCH, TT), F32)[:].rearrange("p a t -> p (a t)")
    lnsq_fl = P.buf("ln_sq", (128, NCH, TT), F32)[:].rearrange("p a t -> p (a t)")
    frow = lnsq_fl[0:16, 0:TS2]
    gcrow = lnsq_fl[0:16, TS2:2 * TS2]
    wsl_v = lnsq_fl[:, 0:1024].bitcast(BF16).rearrange("p (a t) -> p a t", t=256)
    maskf = lnsq_fl[:, 0:2048].rearrange("p (a t) -> p a t", t=512)
    maskb = lnsq_fl[:, 2048:4096].rearrange("p (a t) -> p a t", t=512)
    id_bf = P.buf("idbf_sb", (128, 128), BF16)
    wvf = P.buf("s5_wvf", (128, NCH, 1024 + 16), BF16)
    wvf_fl = wvf[:].rearrange("p a t -> p (a t)")
    wosl = [wvf_fl[:, 0:2048].rearrange("p (a t) -> p a t", t=128),
            wvf_fl[:, 2048:4096].rearrange("p (a t) -> p a t", t=128)]
    xtile = [lnxf_fl[:, 2048:2560], lnxf_fl[:, 2560:3072]]
    orow = P.buf("ones_row", (1, 128), F32)
    oc = P.buf("ones_col", (128, 1), F32)

    t_c1 = P.op("sq", lambda ep: ep.dma_start(out=id_bf[:], in_=w["ident_bf"]), dma=True)
    t_c1 = P.op("sq", lambda ep: ep.dma_start(out=ftm[:][0:1, :], in_=w["bf"]), dma=True)
    t_bfb = P.op("te", lambda ep: ep.matmul(pb[4][:][:, 0:16], orow[:], ftm[:][0:1, :],
                                            start=True, stop=True), after=(t_c1, tok0))
    t_bfb = P.op("ve", lambda ep: ep.tensor_copy(bfb[:], pb[4][:][:, 0:16]), after=(t_bfb,))

    Wq_r = w["WqTb"].rearrange("(a p) m -> p a m", p=128)
    Wk_r = w["WkTb"].rearrange("(a p) m -> p a m", p=128)
    Wv_r = w["WvTb"].rearrange("(a p) m -> p a m", p=128)
    Wf_r = w["WfTb"].rearrange("(a p) m -> p a m", p=128)
    Wo_r = w["WoT2b"].rearrange("(a p) m -> p a m", p=128)

    eps1_ap, eps1_tok = P.cc(1.0)
    t_done = []
    prev_ln = (tok0, t_conv, t_bfb)
    t_wv_cons = None
    t_wsl_cons = None
    t_xt_cons = {0: None, 1: None}
    t_wo_cons = {0: None, 1: None}
    t_bias_cons = None
    t_ssb_cons = None
    t_wv_wf_cons = None
    t_psb_cons = None
    t_pt_cons = {0: None, 1: None}
    t_pb2_free = None
    t_obf_free = None
    t_otp_free = None
    t_oT_guard = None
    obf = None
    id_f = P.buf("ident_f", (128, 128), F32)
    obf = P.buf("ln_xf", (128, NCH, TT), F32)[:].rearrange("p a t -> p (a t)")[:, 3072:3104].bitcast(BF16)
    t_idf = P.op("sq", lambda ep: ep.dma_start(out=id_f[:], in_=w["ident"]), dma=True)

    for s in range(B_LOC):
        t_xn = None
        for jj in range(NJJ):
            t_xn, t_free, _ = emit_ln_tile(
                P, xin_r, s * NJJ + jj, xno[:][:, :, jj * TT:(jj + 1) * TT],
                pb[0], pb[1], after=prev_ln)
            prev_ln = (t_xn, t_free)
        t_wv = P.op("po", lambda ep: ep.dma_start(out=wvf[:][:, :, 0:1024], in_=Wv_r),
                    after=(t_conv, t_wv_cons), dma=True)
        t_wf = P.op("po", lambda ep: ep.dma_start(out=wvf[:][:, :, 1024:1040], in_=Wf_r),
                    after=(t_conv,), dma=True)
        t_vE = None
        t_fT = None
        t_ftm_cons = None
        for st in range(NST):
            for half in range(2):
                for k in range(NCH):
                    P.op("te", lambda ep, st=st, half=half, k=k: ep.matmul(
                        pb[half][:][:, :TT],
                        xno[:][:, k, st * 128:(st + 1) * 128],
                        wvf[:][:, k, half * 512:(half + 1) * 512],
                        start=(k == 0), stop=(k == NCH - 1)),
                        after=(t_xn, t_wv, t_vE), inc=(k == NCH - 1))
            t_vmm = P.token("te")
            t_v0 = P.op("ve", lambda ep, st=st: ep.tensor_copy(
                vtm[:][:, st, 0:512], pb[0][:][:, :TT]), after=(t_vmm,))
            t_vE = P.op("ve", lambda ep, st=st: ep.tensor_copy(
                vtm[:][:, st, 512:1024], pb[1][:][:, :TT]), after=(t_vmm,))
            for k in range(NCH):
                P.op("te", lambda ep, st=st, k=k: ep.matmul(
                    pb[5][:][:, 0:16],
                    xno[:][:, k, st * 128:(st + 1) * 128],
                    wvf[:][:, k, 1024:1040],
                    start=(k == 0), stop=(k == NCH - 1)),
                    after=(t_xn, t_wf, t_ftm_cons), inc=(k == NCH - 1))
            t_fmm = P.token("te")
            t_f1 = P.op("ve", lambda ep: ep.tensor_tensor(ftm[:], pb[5][:][:, 0:16], bfb[:], ALU.add),
                        after=(t_fmm, t_bfb))
            t_f2 = P.op("se", lambda ep: ep.activation(ftm2[:], ftm[:], AF.Abs), after=(t_f1,))
            t_f3 = P.op("se", lambda ep: ep.activation(ftm2[:], ftm2[:], AF.Exp, scale=-1.0), after=(t_f2,))
            t_f4 = P.op("se", lambda ep: ep.activation(ftm2[:], ftm2[:], AF.Ln, bias=eps1_ap),
                        after=(t_f3, eps1_tok))
            t_f5 = P.op("ve", lambda ep: ep.tensor_scalar(ftm[:], ftm[:], 0.0, None, ALU.min), after=(t_f4, t_f2))
            t_f6 = P.op("ve", lambda ep: ep.tensor_tensor(ftm[:], ftm[:], ftm2[:], ALU.subtract), after=(t_f5,))
            t_tp = P.op("te", lambda ep: ep.transpose(pb[5][:][0:16, 0:128], ftm[:], id_f[:]),
                        after=(t_f6, t_fT, t_idf), inc=True)
            t_fT = P.op("ve", lambda ep, st=st: ep.tensor_copy(
                frow[:, st * 128:(st + 1) * 128], pb[5][:][0:16, 0:128]), after=(t_tp,))
            t_ftm_cons = t_fT
        t_wv_wf_cons = P.token("te")
        t_gc = P.op("ve", lambda ep: ep.tensor_tensor_scan(
            gcrow, oc[0:16, :].broadcast_to([16, TS2]), frow, 0.0, ALU.mult, ALU.add),
            after=(t_fT, tok0))
        t_gp0 = P.op("ve", lambda ep: ep.memset(frow[:, 0:1], 0.0), after=(t_gc,))
        t_gp1 = P.op("ve", lambda ep: ep.tensor_copy(frow[:, 1:TS2], gcrow[:, 0:TS2 - 1]),
                     after=(t_gp0,))
        t_gd0 = P.op("po", lambda ep: ep.dma_start(out=w["gcd"][0, :, :], in_=gcrow),
                     after=(t_gc, t_bias_cons, P.token("sq", dma=True)), dma=True)
        t_gd1 = P.op("po", lambda ep: ep.dma_start(out=w["gcd"][1, :, :], in_=frow),
                     after=(t_gp1, t_gd0), dma=True)
        t_qevac = {0: None, 1: None, 2: None, 3: None}
        for m in range(NCH):
            t_wq = P.op("po", lambda ep, m=m: ep.dma_start(
                out=wsl_v[:, :, 0:128], in_=Wq_r[:, :, m * 128:(m + 1) * 128]),
                after=(t_wsl_cons, t_gd1), dma=True)
            t_wk = P.op("po", lambda ep, m=m: ep.dma_start(
                out=wsl_v[:, :, 128:256], in_=Wk_r[:, :, m * 128:(m + 1) * 128]),
                after=(t_wsl_cons,), dma=True)
            for jj in range(NJJ):
                bq = jj % 2
                for k in range(NCH):
                    P.op("te", lambda ep, m=m, jj=jj, k=k, bq=bq: ep.matmul(
                        pb[bq][:][:, :TT], wsl_v[:, k, 0:128],
                        xno[:][:, k, jj * TT:(jj + 1) * TT],
                        start=(k == 0), stop=(k == NCH - 1)),
                        after=(t_wq, t_xn, t_qevac[bq]), inc=(k == NCH - 1))
                t_mm = P.token("te")
                t_qevac[bq] = P.op("ve", lambda ep, m=m, jj=jj, bq=bq: ep.tensor_scalar(
                    qT[:][:, m, jj * TT:(jj + 1) * TT], pb[bq][:][:, :TT], 0.125, None, ALU.mult),
                    after=(t_mm,))
                bk = 2 + jj % 2
                for k in range(NCH):
                    P.op("te", lambda ep, m=m, jj=jj, k=k, bk=bk: ep.matmul(
                        pb[bk][:][:, :TT], wsl_v[:, k, 128:256],
                        xno[:][:, k, jj * TT:(jj + 1) * TT],
                        start=(k == 0), stop=(k == NCH - 1)),
                        after=(t_wk, t_xn, t_qevac[bk]), inc=(k == NCH - 1))
                t_mm2 = P.token("te")
                t_qevac[bk] = P.op("ve", lambda ep, m=m, jj=jj, bk=bk: ep.tensor_copy(
                    kT[:][:, m, jj * TT:(jj + 1) * TT], pb[bk][:][:, :TT]),
                    after=(t_mm2,))
            t_wsl_cons = P.token("te")
        t_proj_done = (P.token("te"), P.token("ve"))
        oT = xno
        t_mk = P.op("sq", lambda ep: ep.dma_start(out=lnsq_fl[:, 0:2048], in_=w["maskf"]),
                    after=t_proj_done + (t_gd1,), dma=True)
        t_mk = P.op("sq", lambda ep: ep.dma_start(out=lnsq_fl[:, 2048:4096], in_=w["maskb"]),
                    after=(t_mk,), dma=True)


        for di, fwd in enumerate((True, False)[:globals().get("S5_DIRS", 2)]):
            for h in range(NH):
                t_bc = (P.op("sq", lambda ep, di=di, h=h: ep.dma_start(
                    out=bias_sb[:], in_=w["gcd"][di, h, :].rearrange("t -> () t").broadcast_to([128, TS2])),
                    after=(t_gd1, t_bias_cons), dma=True),)
                ch, po_ = h // 2, (h % 2) * 64
                for qi in range(NST):
                    kt0, kt1 = (0, qi // 4 + 1) if fwd else (qi // 4, 4)
                    t_sc = []
                    for kt in range(kt0, kt1):
                        pss = pb[(kt - kt0) % 2]
                        t_s1 = P.op("te", lambda ep, h=h, qi=qi, kt=kt, ch=ch, po_=po_, pss=pss: ep.matmul(
                            pss[:][:, :TT],
                            qT[:][po_:po_ + 64, ch, qi * 128:(qi + 1) * 128],
                            kT[:][po_:po_ + 64, ch, kt * TT:(kt + 1) * TT],
                            start=True, stop=True),
                            after=t_proj_done + (t_ssb_cons, t_sc[-2] if len(t_sc) >= 2 else None),
                            inc=True)
                        bop = ALU.subtract if fwd else ALU.add
                        if kt == qi // 4:
                            mk = (maskf if fwd else maskb)[:, qi % 4, :]
                            t_d = P.op("ve", lambda ep, kt=kt, pss=pss, mk=mk: ep.tensor_tensor(
                                ssb[:][:, kt * TT:(kt + 1) * TT], pss[:][:, :TT], mk, ALU.add),
                                after=(t_s1, t_mk))
                            t_d = P.op("ve", lambda ep, kt=kt, bop=bop: ep.tensor_tensor(
                                ssb[:][:, kt * TT:(kt + 1) * TT], ssb[:][:, kt * TT:(kt + 1) * TT],
                                bias_sb[:][:, kt * TT:(kt + 1) * TT], bop), after=(t_d,) + t_bc)
                        else:
                            t_d = P.op("ve", lambda ep, kt=kt, pss=pss, bop=bop: ep.tensor_tensor(
                                ssb[:][:, kt * TT:(kt + 1) * TT], pss[:][:, :TT],
                                bias_sb[:][:, kt * TT:(kt + 1) * TT], bop), after=(t_s1,) + t_bc)
                        t_sc.append(P.op("ve", lambda ep, kt=kt, kk=kt - kt0: ep.tensor_reduce(
                            scols[:][:, kk:kk + 1], ssb[:][:, kt * TT:(kt + 1) * TT],
                            AX.X, ALU.max), after=(t_d,)))
                    nkt = kt1 - kt0
                    t_mx = P.op("ve", lambda ep, nkt=nkt: ep.tensor_reduce(
                        scols[:][:, 6:7], scols[:][:, 0:nkt], AX.X, ALU.max, negate=True),
                        after=tuple(t_sc))
                    t_ex = []
                    for kt in range(kt0, kt1):
                        t_ex.append(P.op("se", lambda ep, kt=kt, kk=kt - kt0: ep.activation(
                            psb[:][:, kt * TT:(kt + 1) * TT], ssb[:][:, kt * TT:(kt + 1) * TT],
                            AF.Exp, bias=scols[:][:, 6:7], accum_out=scols[:][:, kk:kk + 1]),
                            after=(t_mx, t_psb_cons)))
                    t_ex = tuple(t_ex)
                    t_den = P.op("ve", lambda ep, nkt=nkt: ep.tensor_reduce(
                        scols[:][:, 7:8], scols[:][:, 0:nkt], AX.X, ALU.add), after=t_ex)
                    t_den = P.op("ve", lambda ep: ep.reciprocal(scols[:][:, 7:8], scols[:][:, 7:8]),
                                 after=(t_den,))
                    ks0, ks1 = (0, qi + 1) if fwd else (qi, NST)
                    t_pv_mm = None
                    for idx, ks in enumerate(range(ks0, ks1)):
                        bi = idx % 2
                        t_tp = P.op("te", lambda ep, ks=ks, bi=bi: ep.transpose(
                            [pbt, pbt2][bi][:][:, 0:128], psb[:][:, ks * 128:(ks + 1) * 128], id_bf[:]),
                            after=t_ex + (t_pt_cons[bi], t_otp_free))
                        t_cp = P.op("ve", lambda ep, ks=ks, bi=bi: ep.tensor_copy(
                            ptb[:, bi, :], [pbt, pbt2][bi][:][:, 0:128]), after=(t_tp,))
                        t_pv_mm = P.op("te", lambda ep, ks=ks, bi=bi, h=h, idx=idx, last=(ks == ks1 - 1): ep.matmul(
                            pb[2][:][:, 0:HD], ptb[:, bi, :],
                            vtm[:][:, ks, h * HD:(h + 1) * HD],
                            start=(idx == 0), stop=last),
                            after=(t_cp, t_pb2_free), inc=True)
                        t_pt_cons[bi] = t_pv_mm
                    t_psb_cons = P.token("te")
                    t_ssb_cons = tuple(t_ex)[-1] if t_ex else None
                    t_oe = P.op("ve", lambda ep: ep.tensor_scalar(
                        otm, pb[2][:][:, 0:HD], scols[:][:, 7:8], None, ALU.mult),
                        after=(t_pv_mm, t_den))
                    t_pb2_free = t_oe
                    t_oc = P.op("ve", lambda ep: ep.tensor_copy(obf, otm),
                                after=(t_oe, t_obf_free))
                    t_tp2 = P.op("te", lambda ep: ep.transpose(
                        pbt[:][0:HD, 128:256], obf, id_bf[:]),
                        after=(t_oc, t_otp_free))
                    t_obf_free = t_tp2
                    t_otp_free = P.op("ve", lambda ep, h=h, qi=qi, po_=po_: ep.tensor_copy(
                        oT[:][po_:po_ + HD, h // 2, qi * 128:(qi + 1) * 128],
                        pbt[:][0:HD, 128:256]), after=(t_tp2, t_oT_guard))
                t_bias_cons = P.token("te")
            if di >= globals().get("S5_WO_DIRS", 2):
                continue
            t_oT_all = (t_otp_free, P.token("ve"))
            for j2 in range(NCH):
                bi = j2 % 2
                t_w = P.op("po", lambda ep, j2=j2, di=di, bi=bi: ep.dma_start(
                    out=wosl[bi][:, 0:NCH, :],
                    in_=Wo_r[:, di * NCH:(di + 1) * NCH, j2 * 128:(j2 + 1) * 128]),
                    after=(t_wo_cons[bi], t_conv, t_wv_wf_cons), dma=True)
                for jj in range(NJJ):
                    jt = s * NJJ + jj
                    psx = [pb[2], pb[3]][jj % 2]
                    xbi = jj % 2
                    t_xl = P.op("sq", lambda ep, j2=j2, jt=jt, xbi=xbi, di=di: ep.dma_start(
                        out=xtile[xbi], in_=(xin_r if di == 0 else xout_r)[:, j2, jt * TT:(jt + 1) * TT]),
                        after=(t_xt_cons[xbi],), dma=True)
                    for m in range(NCH):
                        P.op("te", lambda ep, jj=jj, m=m, bi=bi, psx=psx: ep.matmul(
                            psx[:][:, :TT], wosl[bi][:, m, :],
                            oT[:][:, m, jj * TT:(jj + 1) * TT],
                            start=(m == 0), stop=(m == NCH - 1)),
                            after=(t_w, t_xt_cons[xbi]) + t_oT_all,
                            inc=(m == NCH - 1))
                    t_mm = P.token("te")
                    t_add = P.op("ve", lambda ep, xbi=xbi, psx=psx: ep.tensor_tensor(
                        xtile[xbi], xtile[xbi], psx[:][:, :TT], ALU.add),
                        after=(t_mm, t_xl))
                    t_st = P.op("se", lambda ep, j2=j2, jt=jt, xbi=xbi: ep.dma_start(
                        out=xout_r[:, j2, jt * TT:(jt + 1) * TT], in_=xtile[xbi]),
                        after=(t_add,), dma=True)
                    t_xt_cons[xbi] = t_st
                    t_done.append(t_st)
                t_wo_cons[bi] = P.token("te")
            t_oT_guard = P.token("te")
            t_pb2_free = P.token("ve")
            t_psb_cons = P.token("te")
        prev_ln = (P.token("te"), P.token("ve"), P.token("se"),
                   P.token("se", dma=True), P.token("sq", dma=True), P.token("po", dma=True))
        t_wv_cons = P.token("te")
    return tuple(t_done)




L0D = 6
FDI = 2730
NJT_A = 22


def emit_l0ffn(P, xin, xout, w, tok0, t_conv):
    xin_r, xout_r = r_cm(xin), r_cm(xout)
    ybuf_r = r_cm(w["ybuf"])
    pb = [P.psum("pb%d" % i) for i in range(6)]
    pbt = P.psum("pbt", (128, 512), BF16)
    pbt2 = P.psum("pbt2", (128, 512), BF16)
    NH, HD = 16, 64
    TS2 = T_SEQ
    NST = TS2 // 128
    NJJ = TS2 // TT
    NBLK = TS2 // 256

    reg = P.buf("l0_reg", (128, 16400), BF16)
    ybf = reg[:][:, 0:8 * 2049].rearrange("p (a t) -> p a t", t=2049)
    maskl = reg[:][:, 0:4096].bitcast(F32).rearrange("p (a t) -> p a t", t=512)
    atT = reg[:][:, 4096:6144].rearrange("p (a t) -> p a t", t=256)
    ssb = reg[:][:, 6144:7168].bitcast(F32)
    psbl = reg[:][:, 7680:8192]
    xnf = reg[:][:, 0:8 * 2048].rearrange("p (a t) -> p a t", t=2048)
    qT = P.buf("l0_qT", (128, NCH, TS2), BF16)
    kT = P.buf("l0_kT", (128, NCH, TS2), BF16)
    vtm = P.buf("l0_vtm", (128, NST, 1024), BF16)
    qT_fl = qT[:].rearrange("p a t -> p (a t)")
    hbuf = qT_fl[:, 0:NJT_A * TT].rearrange("p (a t) -> p a t", t=TT)
    vtm_fl = vtm[:].rearrange("p a t -> p (a t)")
    w2resB = vtm_fl[:, 0:6 * 1024].rearrange("p (a t) -> p a t", t=1024)
    w1sl = [vtm_fl[:, 6144 + i * 4096:6144 + (i + 1) * 4096].rearrange("p (a t) -> p a t", t=512)
            for i in range(2)]
    w2resA = kT[:].rearrange("p a t -> p (a t)").rearrange("p (a t) -> p a t", t=1024)
    wslq = [P.buf("l0_wq0", (128, NCH, 512), BF16), P.buf("l0_wq1", (128, NCH, 512), BF16)]
    wores = P.buf("l0_wo", (128, NCH, 1024), BF16)
    qsb = P.buf("l0_qsb", (128, 4, 64), F32)
    lnxf = P.buf("ln_xf", (128, NCH, TT), F32)
    lnxf_fl = lnxf[:].rearrange("p a t -> p (a t)")
    qtm = lnxf_fl[:, 0:1024]
    sqb = lnxf_fl[:, 1024:2048]
    swb = lnxf_fl[:, 2048:3072]
    lnsq = P.buf("ln_sq", (128, NCH, TT), F32)
    ftab = lnsq[:].rearrange("p a t -> p (a t)").rearrange("p (a t) -> p a t", t=1024)
    ssc = P.buf("l0_ssc", (128, 24), F32)
    P.ln_tmp_alias = P.buf("l0_wq1", (128, NCH, 512), BF16)[:].rearrange(
        "p a t -> p (a t)").bitcast(F32)[:, 0:TT]
    P.ln_rows_alias = P.buf("l0_wq0", (128, NCH, 512), BF16)[:].rearrange(
        "p a t -> p (a t)").bitcast(F32)[:, 0:3 * TT]
    id_bf = P.buf("idbf_sb", (128, 128), BF16)
    ptb = P.buf("l0_ptb", (128, 2, 128), BF16)
    obf = P.buf("l0_obf", (128, 1024), BF16)
    ytile = [P.buf("l0_yt0", (128, TT), F32), P.buf("l0_yt1", (128, TT), F32)]
    gcol = P.buf("l0_gcol", (128, 2 * NCH + 2 * 43 + NCH), F32)

    t_id = P.op("sq", lambda ep: ep.dma_start(out=id_bf[:], in_=w["ident_bf"]), dma=True)

    Wq_r = w["Wqkvb"].rearrange("(a p) m -> p a m", p=128)
    Wo_r = w["Wob"].rearrange("(a p) m -> p a m", p=128)
    W1_r = w["W1b"].rearrange("(a p) m -> p a m", p=128)

    t_done = []
    t_prev = (tok0, t_conv, t_id)

    y0v = lnsq[:]
    t_y0 = None
    for j in range(NTT):
        t_xn, t_free, _ = emit_ln_tile(P, xin_r, j, y0v, pb[0], pb[1],
                                       after=t_prev + (t_y0,))
        t_y0 = P.op("po", lambda ep, j=j: ep.dma_start(
            out=ybuf_r[:, :, j * TT:(j + 1) * TT], in_=y0v), after=(t_xn,), dma=True)
        t_prev = (t_xn, t_free, t_y0)
    P.barrier()

    for li in range(L0D):
        t_qs = P.op("sq", lambda ep, li=li: ep.dma_start(
            out=qsb[:].rearrange("p a d -> p (a d)"),
            in_=w["qscat"][li, :].rearrange("t -> () t").broadcast_to([128, 256])), dma=True)
        t_wo = P.op("po", lambda ep, li=li: ep.dma_start(
            out=wores[:], in_=Wo_r[:, li * NCH:(li + 1) * NCH, :]), after=(t_conv,), dma=True)
        t_gc = P.op("sq", lambda ep, li=li: ep.dma_start(
            out=gcol[:], in_=w["percol"][li, :, :]), dma=True)
        lng_c = gcol[:][:, 0:NCH]
        lnb_c = gcol[:][:, NCH:2 * NCH]
        b1a_c = gcol[:][:, 2 * NCH:2 * NCH + 43]
        b1g_c = gcol[:][:, 2 * NCH + 43:2 * NCH + 86]
        b2_c = gcol[:][:, 2 * NCH + 86:3 * NCH + 86]

        for si in range(B_LOC):
            t0 = si * TS2
            P.barrier()
            t_z = P.op("ve", lambda ep: ep.memset(ybf[:, :, 0:1], 0.0))
            t_ybf = t_z
            for jj in range(NJJ):
                t_l = P.op("sq", lambda ep, jj=jj, t0=t0: ep.dma_start(
                    out=lnxf[:], in_=ybuf_r[:, :, t0 + jj * TT:t0 + (jj + 1) * TT]),
                    after=(t_ybf,), dma=True)
                t_ybf = P.op("ve", lambda ep, jj=jj: ep.tensor_copy(
                    ybf[:, :, 1 + jj * TT:1 + (jj + 1) * TT], lnxf[:]),
                    after=(t_l, t_z))
            P.barrier()
            t_ftab = []
            for ti, nm in enumerate(("tqc", "tqs", "tkc", "tks")):
                tl = P.op("sq", lambda ep, ti=ti, nm=nm: ep.dma_start(
                    out=ftab[:, ti, :], in_=w[nm].rearrange("p a d -> p (a d)")), dma=True)
                qv = qsb[:][:, ti, :].rearrange("p d -> p () d").broadcast_to([128, 16, HD])
                t_ftab.append(P.op("ve", lambda ep, ti=ti, qv=qv: ep.tensor_tensor(
                    ftab[:, ti, :].rearrange("p (s d) -> p s d", d=HD),
                    ftab[:, ti, :].rearrange("p (s d) -> p s d", d=HD), qv, ALU.mult),
                    after=(tl, t_qs)))
            t_ftab = tuple(t_ftab)
            t_wsl_cons = {0: None, 1: None}
            t_ps_evac = {0: None, 1: None}
            t_pt_cons = {0: None, 1: None}
            t_qkT_cons = None
            t_obf_cons = None
            t_qtm_cons = None
            for grp in range(6):
                gb_ = grp % 2
                t_wl = P.op("po", lambda ep, li=li, grp=grp, gb_=gb_: ep.dma_start(
                    out=wslq[gb_][:], in_=Wq_r[:, li * NCH:(li + 1) * NCH, grp * 512:(grp + 1) * 512]),
                    after=(t_conv, t_wsl_cons[gb_]), dma=True)
                for st in range(NST):
                    pp = st % 2
                    for k in range(NCH):
                        P.op("te", lambda ep, st=st, k=k, pp=pp, gb_=gb_: ep.matmul(
                            pb[pp][:][:, :TT],
                            ybf[:, k, 1 + st * 128:1 + (st + 1) * 128] if k < 4
                            else ybf[:, k, st * 128:(st + 1) * 128],
                            wslq[gb_][:][:, k, :],
                            start=(k == 0), stop=(k == NCH - 1)),
                            after=(t_wl, t_ps_evac[pp]), inc=(k == NCH - 1))
                    t_mm = P.token("te")
                    if grp >= 4:
                        t_e = P.op("ve", lambda ep, st=st, grp=grp, pp=pp: ep.tensor_copy(
                            vtm[:][:, st, (grp - 4) * 512:(grp - 3) * 512], pb[pp][:][:, :TT]),
                            after=(t_mm,))
                    else:
                        isq = grp < 2
                        t_c0 = P.op("ve", lambda ep, pp=pp: ep.tensor_copy(qtm[:, 0:512], pb[pp][:][:, :TT]),
                                    after=(t_mm, t_qtm_cons))
                        t_sq = P.op("se", lambda ep: ep.activation(sqb[:, 0:512], qtm[:, 0:512], AF.Square),
                                    after=(t_c0,))
                        t_r1 = P.op("ve", lambda ep: ep.reduce_sum(
                            ssc[:][:, 0:8], sqb[:, 0:512].rearrange("p (h d) -> p h d", d=HD), AX.X),
                            after=(t_sq,))
                        t_r2 = P.op("ve", lambda ep: ep.tensor_scalar(
                            ssc[:][:, 0:8], ssc[:][:, 0:8], 1e-24, None, ALU.max), after=(t_r1,))
                        t_r3 = P.op("se", lambda ep: ep.activation(ssc[:][:, 0:8], ssc[:][:, 0:8], AF.Ln),
                                    after=(t_r2,))
                        t_r4 = P.op("se", lambda ep: ep.activation(ssc[:][:, 0:8], ssc[:][:, 0:8], AF.Exp,
                                                                   scale=-0.5), after=(t_r3,))
                        rnb = ssc[:][:, 0:8].rearrange("p h -> p h ()").broadcast_to([128, 8, HD])
                        q3 = qtm[:, 0:512].rearrange("p (h d) -> p h d", d=HD)
                        g3 = swb[:, 0:512].rearrange("p (h d) -> p h d", d=HD)
                        t_n = P.op("ve", lambda ep, q3=q3, rnb=rnb: ep.tensor_tensor(q3, q3, rnb, ALU.mult),
                                   after=(t_r4, t_c0))
                        ti = 0 if isq else 2
                        cst = ftab[:, ti, st * 64:(st + 1) * 64].rearrange("p d -> p () d").broadcast_to([128, 8, HD])
                        sst = ftab[:, ti + 1, st * 64:(st + 1) * 64].rearrange("p d -> p () d").broadcast_to([128, 8, HD])
                        t_s1_ = P.op("ve", lambda ep, g3=g3, q3=q3: ep.tensor_copy(
                            g3[:, :, 0:32], q3[:, :, 32:64]), after=(t_n,))
                        t_s2_ = P.op("ve", lambda ep, g3=g3, q3=q3: ep.tensor_copy(
                            g3[:, :, 32:64], q3[:, :, 0:32]), after=(t_n, t_s1_))
                        t_s3_ = P.op("ve", lambda ep, g3=g3, sst=sst: ep.tensor_tensor(g3, g3, sst, ALU.mult),
                                     after=(t_s2_,) + t_ftab)
                        t_s4_ = P.op("ve", lambda ep, q3=q3, cst=cst: ep.tensor_tensor(q3, q3, cst, ALU.mult),
                                     after=(t_n,) + t_ftab)
                        t_s5_ = P.op("ve", lambda ep, q3=q3, g3=g3: ep.tensor_tensor(
                            obf[:][:, 0:512].rearrange("p (h d) -> p h d", d=HD), q3, g3, ALU.add),
                            after=(t_s3_, t_s4_, t_obf_cons))
                        t_qtm_cons = t_s4_
                        dstT = qT if isq else kT
                        t_e = None
                        for hh in range(8):
                            h = (grp % 2) * 8 + hh
                            pt = [pbt, pbt2][hh % 2]
                            t_tp = P.op("te", lambda ep, hh=hh, pt=pt: ep.transpose(
                                pt[:][0:64, 0:128], obf[:][:, hh * 64:(hh + 1) * 64], id_bf[:]),
                                after=(t_s5_, t_pt_cons[hh % 2], t_id))
                            t_e = P.op("ve", lambda ep, h=h, st=st, pt=pt, dstT=dstT: ep.tensor_copy(
                                dstT[:][(h % 2) * 64:(h % 2) * 64 + 64, h // 2, st * 128:(st + 1) * 128],
                                pt[:][0:64, 0:128]), after=(t_tp,))
                            t_pt_cons[hh % 2] = t_e
                        t_obf_cons = P.token("te")
                    t_ps_evac[pp] = t_e
                t_wsl_cons[gb_] = P.token("te")
            P.barrier()
            t_mk = P.op("sq", lambda ep: ep.dma_start(
                out=maskl.rearrange("p a t -> p (a t)").bitcast(BF16),
                in_=w["maskl"].bitcast(BF16)), dma=True)
            t_sc_prev = None
            t_ssb_cons = None
            t_psb_cons = None
            t_pb2_free = None
            t_obf_cons = None
            t_atT_cons = None
            t_pt_cons = {0: None, 1: None}
            t_yl_pv = {0: None, 1: None}
            for b in range(NBLK):
                kw0 = max(0, (b - 1) * 256)
                kwid = (b + 1) * 256 - kw0
                t_oe_all = []
                for sub in range(2):
                    for h in range(NH):
                        ch, po_ = h // 2, (h % 2) * 64
                        q0 = b * 256 + sub * 128
                        t_s1 = P.op("te", lambda ep, h=h, q0=q0, kw0=kw0, kwid=kwid, ch=ch, po_=po_: ep.matmul(
                            pb[4][:][:, 0:kwid],
                            qT[:][po_:po_ + 64, ch, q0:q0 + 128],
                            kT[:][po_:po_ + 64, ch, kw0:kw0 + kwid],
                            start=True, stop=True),
                            after=(t_sc_prev, t_ssb_cons), inc=True)
                        mi = sub + (2 if b == 0 else 0)
                        t_d = P.op("ve", lambda ep, mi=mi, kwid=kwid: ep.tensor_tensor(
                            ssb[:, 0:kwid], pb[4][:][:, 0:kwid],
                            maskl[:, mi, 512 - kwid:512], ALU.add),
                            after=(t_s1, t_mk))
                        t_sc_prev = t_d
                        t_mx = P.op("ve", lambda ep, kwid=kwid: ep.tensor_reduce(
                            ssc[:][:, 16:17], ssb[:, 0:kwid], AX.X, ALU.max, negate=True),
                            after=(t_d,))
                        t_ex = P.op("se", lambda ep, kwid=kwid: ep.activation(
                            psbl[:, 0:kwid], ssb[:, 0:kwid], AF.Exp,
                            bias=ssc[:][:, 16:17], accum_out=ssc[:][:, 17:18]),
                            after=(t_mx, t_psb_cons))
                        t_ssb_cons = t_ex
                        t_rc = P.op("ve", lambda ep: ep.reciprocal(ssc[:][:, 17:18], ssc[:][:, 17:18]),
                                    after=(t_ex,))
                        nks = kwid // 128
                        t_pv = None
                        for ik in range(nks):
                            bi = ik % 2
                            pt = [pbt, pbt2][bi]
                            t_tp = P.op("te", lambda ep, ik=ik, pt=pt: ep.transpose(
                                pt[:][:, 0:128], psbl[:, ik * 128:(ik + 1) * 128], id_bf[:]),
                                after=(t_ex, t_pt_cons[bi]))
                            t_cp = P.op("ve", lambda ep, bi=bi, pt=pt: ep.tensor_copy(
                                ptb[:][:, bi, :], pt[:][:, 0:128]), after=(t_tp,))
                            t_pv = P.op("te", lambda ep, ik=ik, bi=bi, h=h, kw0=kw0, last=(ik == nks - 1): ep.matmul(
                                pb[2][:][:, 0:HD], ptb[:][:, bi, :],
                                vtm[:][:, kw0 // 128 + ik, h * HD:(h + 1) * HD],
                                start=(ik == 0), stop=last),
                                after=(t_cp, t_pb2_free), inc=True)
                            t_pt_cons[bi] = t_pv
                        t_psb_cons = P.token("te")
                        t_oe = P.op("ve", lambda ep, h=h: ep.tensor_scalar(
                            obf[:][:, h * 64:(h + 1) * 64], pb[2][:][:, 0:HD],
                            ssc[:][:, 17:18], None, ALU.mult),
                            after=(t_pv, t_rc, t_obf_cons))
                        t_pb2_free = t_oe
                        t_oe_all.append(t_oe)
                    t_at = []
                    for a in range(NCH):
                        pt = [pbt, pbt2][a % 2]
                        t_tp = P.op("te", lambda ep, a=a, pt=pt: ep.transpose(
                            pt[:][:, 0:128], obf[:][:, a * 128:(a + 1) * 128], id_bf[:]),
                            after=tuple(t_oe_all) + (t_pt_cons[a % 2], t_atT_cons))
                        t_at.append(P.op("ve", lambda ep, a=a, sub=sub, pt=pt: ep.tensor_copy(
                            atT[:, a, sub * 128:(sub + 1) * 128], pt[:][:, 0:128]),
                            after=(t_tp,)))
                        t_pt_cons[a % 2] = t_at[-1]
                    t_obf_cons = P.token("te")
                    t_atT_last = tuple(t_at)
                for m in range(NCH):
                    pw = m % 2
                    for k in range(NCH):
                        P.op("te", lambda ep, m=m, k=k, pw=pw: ep.matmul(
                            pb[pw][:][:, 0:256], wores[:][:, k, m * 128:(m + 1) * 128],
                            atT[:, k, 0:256], start=(k == 0), stop=(k == NCH - 1)),
                            after=(t_wo, t_yl_pv[pw]) + t_atT_last, inc=(k == NCH - 1))
                    t_mm = P.token("te")
                    t_yl = P.op("sq", lambda ep, m=m, b=b, t0=t0, pw=pw: ep.dma_start(
                        out=ytile[pw][:][:, 0:256], in_=ybuf_r[:, m, t0 + b * 256:t0 + (b + 1) * 256]),
                        after=(t_yl_pv[pw],), dma=True)
                    t_ad = P.op("ve", lambda ep, pw=pw: ep.tensor_tensor(
                        ytile[pw][:][:, 0:256], ytile[pw][:][:, 0:256], pb[pw][:][:, 0:256], ALU.add),
                        after=(t_mm, t_yl))
                    t_st = P.op("se", lambda ep, m=m, b=b, t0=t0, pw=pw: ep.dma_start(
                        out=ybuf_r[:, m, t0 + b * 256:t0 + (b + 1) * 256], in_=ytile[pw][:][:, 0:256]),
                        after=(t_ad,), dma=True)
                    t_yl_pv[pw] = t_st
                t_atT_cons = P.token("te")
            P.barrier()
            t_xn_all = []
            t_prev_ln = ()
            for jt in range(NJJ):
                def loader(xf, after, jt=jt, t0=t0):
                    ta = P.op("sq", lambda ep, jt=jt, t0=t0: ep.dma_start(
                        out=xf[:][:, 0:4, :], in_=ybuf_r[:, 0:4, t0 + jt * TT:t0 + (jt + 1) * TT]),
                        after=after, dma=True)
                    if jt == 0:
                        tz = P.op("ve", lambda ep: ep.memset(xf[:][:, 4:8, 0:1], 0.0), after=after)
                        tb = P.op("sq", lambda ep, t0=t0: ep.dma_start(
                            out=xf[:][:, 4:8, 1:TT], in_=ybuf_r[:, 4:8, t0:t0 + TT - 1]),
                            after=after + (ta,), dma=True)
                        tw = P.op("ve", lambda ep: ep.tensor_copy(
                            ssc[:][:, 20:21], xf[:][:, 0, 0:1]), after=(ta, tb, tz))
                    else:
                        tb = P.op("sq", lambda ep, jt=jt, t0=t0: ep.dma_start(
                            out=xf[:][:, 4:8, :], in_=ybuf_r[:, 4:8, t0 + jt * TT - 1:t0 + (jt + 1) * TT - 1]),
                            after=after + (ta,), dma=True)
                        tw = P.op("ve", lambda ep: ep.tensor_copy(
                            ssc[:][:, 20:21], xf[:][:, 0, 0:1]), after=(ta, tb))
                    return tw
                t_xn, t_free, _ = emit_ln_tile(
                    P, None, jt, xnf[:, :, jt * TT:(jt + 1) * TT], pb[0], pb[1],
                    after=t_prev_ln, gb=(lng_c, lnb_c), loader=loader)
                t_prev_ln = (t_xn, t_free)
                t_xn_all.append(t_xn)
            t_xn_all = tuple(t_xn_all)
            P.barrier()
            t_w2r = P.op("po", lambda ep, li=li: ep.dma_start(
                out=w2resA,
                in_=w["W2b"][li * 2816:li * 2816 + 2048, :].rearrange("(a p) m -> p a m", p=128)),
                dma=True)
            t_w2r2 = P.op("po", lambda ep, li=li: ep.dma_start(
                out=w2resB[:, 0:5, :],
                in_=w["W2b"][li * 2816 + 2048:li * 2816 + 2688, :].rearrange("(a p) m -> p a m", p=128)),
                dma=True)
            t_w2r2 = P.op("po", lambda ep, li=li: ep.dma_start(
                out=w2resB[0:42, 5, :],
                in_=w["W2b"][li * 2816 + 2688:li * 2816 + 2730, :]),
                after=(t_w2r2,), dma=True)
            t_w1_cons = {0: None, 1: None}
            t_w1free = {0: None, 1: None, 2: None, 3: None}
            t_hb_cons = None
            t_yt_cons = {0: None, 1: None}
            t_geb_cons = None
            for jt in range(NJJ):
                t_h = []
                for jp in range(11):
                    wb = jp % 2
                    a0 = jp * 256
                    aw = min(256, FDI - a0)
                    t_wl1 = P.op("po", lambda ep, li=li, jp=jp, wb=wb, a0=a0, aw=aw: ep.dma_start(
                        out=w1sl[wb][:, :, 0:aw], in_=W1_r[:, li * NCH:(li + 1) * NCH, a0:a0 + aw]),
                        after=(t_w1_cons[wb],), dma=True)
                    t_wl2 = P.op("po", lambda ep, li=li, jp=jp, wb=wb, a0=a0, aw=aw: ep.dma_start(
                        out=w1sl[wb][:, :, 256:256 + aw],
                        in_=W1_r[:, li * NCH:(li + 1) * NCH, FDI + a0:FDI + a0 + aw]),
                        after=(t_wl1,), dma=True)
                    for jh in range(2):
                        j = jp * 2 + jh
                        jw = min(128, FDI - j * 128)
                        if jw <= 0:
                            continue
                        pa, pg = pb[jh], pb[2 + jh]
                        for k in range(NCH):
                            P.op("te", lambda ep, j=j, jh=jh, k=k, jw=jw, wb=wb, pa=pa, jt=jt: ep.matmul(
                                pa[:][0:jw, :TT], w1sl[wb][:, k, jh * 128:jh * 128 + jw],
                                xnf[:, k, jt * TT:(jt + 1) * TT],
                                start=(k == 0), stop=(k == NCH - 1)),
                                after=(t_wl2, t_hb_cons, t_w1free[jh]) + t_xn_all, inc=(k == NCH - 1))
                        for k in range(NCH):
                            P.op("te", lambda ep, j=j, jh=jh, k=k, jw=jw, wb=wb, pg=pg, jt=jt: ep.matmul(
                                pg[:][0:jw, :TT], w1sl[wb][:, k, 256 + jh * 128:256 + jh * 128 + jw],
                                xnf[:, k, jt * TT:(jt + 1) * TT],
                                start=(k == 0), stop=(k == NCH - 1)),
                                after=(t_wl2, t_w1free[2 + jh]), inc=(k == NCH - 1))
                        t_mm = P.token("te")
                        t_ge = P.op("se", lambda ep, j=j, jw=jw, pg=pg: ep.activation(
                            sqb[0:jw, 0:TT] if False else lnxf_fl[0:jw, 1024:1536],
                            pg[:][0:jw, :TT], AF.Gelu, bias=b1g_c[:, j:j + 1][0:jw, :]),
                            after=(t_mm, t_geb_cons, t_gc))
                        t_hv = P.op("ve", lambda ep, j=j, jw=jw, pa=pa: ep.scalar_tensor_tensor(
                            hbuf[0:jw, j, :], pa[:][0:jw, :TT], b1a_c[:, j:j + 1][0:jw, :],
                            lnxf_fl[0:jw, 1024:1536], ALU.add, ALU.mult),
                            after=(t_ge, t_hb_cons))
                        t_geb_cons = t_hv
                        t_w1free[jh] = t_hv
                        t_w1free[2 + jh] = t_ge
                        t_h.append(t_hv)
                    t_w1_cons[wb] = P.token("te")
                t_h = tuple(t_h)
                for m in range(NCH):
                    pw2 = pb[4] if m % 2 == 0 else pb[5]
                    for kc in range(NJT_A):
                        rows = min(128, FDI - kc * 128)
                        lhs = (w2resA[:, kc, m * 128:(m + 1) * 128] if kc < 16
                               else w2resB[:, kc - 16, m * 128:(m + 1) * 128])
                        P.op("te", lambda ep, m=m, kc=kc, rows=rows, lhs=lhs, pw2=pw2: ep.matmul(
                            pw2[:][:, :TT], lhs[0:rows, :], hbuf[0:rows, kc, :],
                            start=(kc == 0), stop=(kc == NJT_A - 1)),
                            after=(t_w2r, t_w2r2, t_yt_cons[m % 2]) + t_h, inc=(kc == NJT_A - 1))
                    t_mm2 = P.token("te")
                    t_yl = P.op("sq", lambda ep, m=m, jt=jt, t0=t0: ep.dma_start(
                        out=ytile[m % 2][:], in_=ybuf_r[:, m, t0 + jt * TT:t0 + (jt + 1) * TT]),
                        after=(t_yt_cons[m % 2],), dma=True)
                    t_ad = P.op("ve", lambda ep, m=m, pw2=pw2: ep.scalar_tensor_tensor(
                        ytile[m % 2][:], pw2[:][:, :TT], b2_c[:, m:m + 1], ytile[m % 2][:],
                        ALU.add, ALU.add), after=(t_mm2, t_yl, t_gc))
                    t_st = P.op("se", lambda ep, m=m, jt=jt, t0=t0: ep.dma_start(
                        out=ybuf_r[:, m, t0 + jt * TT:t0 + (jt + 1) * TT], in_=ytile[m % 2][:]),
                        after=(t_ad,), dma=True)
                    t_yt_cons[m % 2] = t_st
                t_hb_cons = P.token("te")
            P.barrier()
        t_wo_cons = P.token("te")
        t_ft_cons = P.token("ve")
        t_gc_cons = P.token("ve")
    P.barrier()
    t_xy = {0: None, 1: None}
    for a in range(NCH):
        for jt in range(NTT):
            pz = (a * NTT + jt) % 2
            t_lx = P.op("sq", lambda ep, a=a, jt=jt, pz=pz: ep.dma_start(
                out=ytile[pz][:], in_=xin_r[:, a, jt * TT:(jt + 1) * TT]),
                after=(t_xy[pz],), dma=True)
            t_ly = P.op("po", lambda ep, a=a, jt=jt, pz=pz: ep.dma_start(
                out=lnxf[:][:, 4 * pz:4 * pz + 1, :].rearrange("p a t -> p (a t)"),
                in_=ybuf_r[:, a, jt * TT:(jt + 1) * TT]),
                after=(t_xy[pz],), dma=True)
            t_ad = P.op("ve", lambda ep, pz=pz: ep.tensor_tensor(
                ytile[pz][:], ytile[pz][:], lnxf[:][:, 4 * pz:4 * pz + 1, :].rearrange("p a t -> p (a t)"),
                ALU.add), after=(t_lx, t_ly))
            t_st = P.op("se", lambda ep, a=a, jt=jt, pz=pz: ep.dma_start(
                out=xout_r[:, a, jt * TT:(jt + 1) * TT], in_=ytile[pz][:]),
                after=(t_ad,), dma=True)
            t_xy[pz] = t_st
            t_done.append(t_st)
    return tuple(t_done)


def build_s2():
    nc = bass.Bass("TRN2", target_bir_lowering=False)
    xin = nc.dram_tensor("x_in", [D, NT], F32, kind="ExternalInput").ap()
    w = {}
    w["Wqkv"] = nc.dram_tensor("Wqkv", [L0D * D, 3 * D], F32, kind="ExternalInput").ap()
    w["Wo"] = nc.dram_tensor("Wo", [L0D * D, D], F32, kind="ExternalInput").ap()
    w["W1"] = nc.dram_tensor("W1", [L0D * D, 2 * FDI], F32, kind="ExternalInput").ap()
    w["W2"] = nc.dram_tensor("W2", [L0D * FDI, D], F32, kind="ExternalInput").ap()
    w["Wqkvb"] = nc.dram_tensor("Wqkvb", [L0D * D, 3 * D], BF16, kind="Internal").ap()
    w["Wob"] = nc.dram_tensor("Wob", [L0D * D, D], BF16, kind="Internal").ap()
    w["W1b"] = nc.dram_tensor("W1b", [L0D * D, 2 * FDI], BF16, kind="Internal").ap()
    w["W2b"] = nc.dram_tensor("W2b", [L0D * 2816, D], BF16, kind="Internal").ap()
    w["qscat"] = nc.dram_tensor("qscat", [L0D, 256], F32, kind="ExternalInput").ap()
    w["percol"] = nc.dram_tensor("percol", [L0D, 128, 2 * NCH + 2 * 43 + NCH], F32, kind="ExternalInput").ap()
    for nm in ("tqc", "tqs", "tkc", "tks"):
        w[nm] = nc.dram_tensor(nm, [128, 16, 64], F32, kind="ExternalInput").ap()
    w["maskl"] = nc.dram_tensor("maskl", [128, 4 * 512], F32, kind="ExternalInput").ap()
    w["ident_bf"] = nc.dram_tensor("ident_bf", [128, 128], BF16, kind="ExternalInput").ap()
    w["ybuf"] = nc.dram_tensor("ybuf", [D, NT], F32, kind="Internal").ap()
    xout = nc.dram_tensor("x_out", [D, NT], F32, kind="ExternalOutput").ap()
    P = Prog(nc)
    tok0 = emit_const_init(P)[-1]
    st = {}
    for li in range(L0D):
        emit_conv_w(P, w["Wqkv"][li * D:(li + 1) * D, :], w["Wqkvb"][li * D:(li + 1) * D, :], tok0, st)
        emit_conv_w(P, w["Wo"][li * D:(li + 1) * D, :], w["Wob"][li * D:(li + 1) * D, :], tok0, st)
        emit_conv_w(P, w["W1"][li * D:(li + 1) * D, :], w["W1b"][li * D:(li + 1) * D, :], tok0, st)
        t_conv = emit_conv_w(P, w["W2"][li * FDI:(li + 1) * FDI, :], w["W2b"][li * 2816:li * 2816 + FDI, :], tok0, st)
    toks = emit_l0ffn(P, xin, xout, w, tok0, t_conv)
    _finish(P, toks)
    P.realize()
    return nc

def build_s5():
    nc = bass.Bass("TRN2", target_bir_lowering=False)
    xin = nc.dram_tensor("x_in", [D, NT], F32, kind="ExternalInput").ap()
    w = {}
    for n, shp in (("WqT", [D, D]), ("WkT", [D, D]), ("WvT", [D, D]),
                   ("WfT", [D, 16]), ("WoT2", [2 * D, D])):
        w[n] = nc.dram_tensor(n, shp, F32, kind="ExternalInput").ap()
        w[n + "b"] = nc.dram_tensor(n + "b", shp, BF16, kind="Internal").ap()
    w["gcd"] = nc.dram_tensor("gcd", [2, 16, T_SEQ], F32, kind="Internal").ap()
    w["maskf"] = nc.dram_tensor("maskf", [128, 4 * 512], F32, kind="ExternalInput").ap()
    w["maskb"] = nc.dram_tensor("maskb", [128, 4 * 512], F32, kind="ExternalInput").ap()
    w["bf"] = nc.dram_tensor("bf", [1, 16], F32, kind="ExternalInput").ap()
    w["ident"] = nc.dram_tensor("ident", [128, 128], F32, kind="ExternalInput").ap()
    w["ident_bf"] = nc.dram_tensor("ident_bf", [128, 128], BF16, kind="ExternalInput").ap()
    xout = nc.dram_tensor("x_out", [D, NT], F32, kind="ExternalOutput").ap()
    P = Prog(nc)
    tok0 = emit_const_init(P)[-1]
    st = {}
    for n in ("WqT", "WkT", "WvT", "WfT", "WoT2"):
        t_conv = emit_conv_w(P, w[n], w[n + "b"], tok0, st)
    toks = emit_s5(P, xin, xout, w, tok0, t_conv)
    _finish(P, toks)
    P.realize()
    return nc


def emit_s7(P, xin, xout, w, tok0):
    P.ln_tmp_alias = None
    P.ln_rows_alias = None
    """w: {"WT": AP [1024, 1024] (= W.T), "b_col": AP [128, 8]}."""
    xin_r, xout_r = r_cm(xin), r_cm(xout)
    ps_a, ps_b, ps_c = P.psum("pb0"), P.psum("pb1"), P.psum("pb2")
    xn = P.buf("s7_xn", (128, NCH, TT), BF16)
    b_col = P.buf("s7_b", (128, NCH), F32)
    vv = P.buf("s7_v", (128, 2 * NCH), F32)
    wt, t_w = load_w_bf16(P, "s7_wt", w["WT"], after=(tok0,))
    t_b = P.op("sq", lambda e: e.dma_start(out=b_col[:], in_=w["b_col"]), dma=True)

    t_v = []
    prev_ln = (tok0,)
    for s in range(B_LOC):
        j = s * (T_SEQ // TT)
        t_xn, t_free, _ = emit_ln_tile(P, xin_r, j, xn[:][:, :, :], ps_a, ps_b, after=prev_ln)
        for m in range(NCH):
            for k in range(NCH):
                P.op("te", lambda e, m=m, k=k, s=s: e.matmul(
                    ps_c[:][:, s * NCH + m:s * NCH + m + 1],
                    wt[:][:, k, m * 128:(m + 1) * 128],
                    xn[:][:, k, 1:2],
                    start=(k == 0), stop=(k == NCH - 1)),
                    after=(t_xn, t_w), inc=(k == NCH - 1))
        t_v.append(P.token("te"))
        prev_ln = (t_xn, t_free, t_v[-1])
    t_vv0 = P.op("ve", lambda e: e.tensor_tensor(vv[:][:, 0:NCH], ps_c[:][:, 0:NCH], b_col[:], ALU.add),
                 after=(t_v[0], t_b))
    t_vv1 = P.op("ve", lambda e: e.tensor_tensor(vv[:][:, NCH:2 * NCH], ps_c[:][:, NCH:2 * NCH], b_col[:], ALU.add),
                 after=(t_v[1], t_b))
    t_vv = (t_vv0, t_vv1)

    xs = [P.buf("s7_xs0", (128, NCH, TT), F32), P.buf("s7_xs1", (128, NCH, TT), F32)]
    t_done = []
    t_store = {0: None, 1: None}
    for j in range(NTT):
        bi = j % 2
        s = j // (T_SEQ // TT)
        t_l = P.op("sq", lambda e, j=j, bi=bi: e.dma_start(
            out=xs[bi][:], in_=xin_r[:, :, j * TT:(j + 1) * TT]),
            after=t_vv + (t_store[bi],), dma=True)
        prev = None
        for a in range(NCH):
            prev = P.op("ve", lambda e, j=j, a=a, bi=bi, s=s: e.tensor_scalar(
                xs[bi][:][:, a, :], xs[bi][:][:, a, :], vv[:][:, s * NCH + a:s * NCH + a + 1],
                None, ALU.add),
                after=(t_l, prev))
        t_store[bi] = P.op("po", lambda e, j=j, bi=bi: e.dma_start(
            out=xout_r[:, :, j * TT:(j + 1) * TT], in_=xs[bi][:]),
            after=(prev,), dma=True)
        t_done.append(t_store[bi])
    return tuple(t_done)



def emit_s1(P, xin, xout, w, tok0):
    P.ln_tmp_alias = None
    P.ln_rows_alias = None
    """w: WkT/WvT/WrT/WoT [1024,1024] fp32 (pre-transposed), td/tf/tm [128,8]."""
    nc = P.nc
    xin_r, xout_r = r_cm(xin), r_cm(xout)
    ps_a, ps_b = P.psum("pb0"), P.psum("pb1")
    ps_c, ps_d = P.psum("pb2"), P.psum("pb3")
    TS2 = T_SEQ
    xn_s = P.buf("s1_xn", (128, NCH, TS2), BF16)
    xm_s = P.buf("s1_xm", (128, NCH, TS2), BF16)
    rwkv = P.buf("s1_rwkv", (128, NCH, TS2), BF16)
    scf = P.buf("scf", (128, 8192), F32)
    kbuf = scf[:][:, 0:TS2]
    kvb = scf[:][:, TS2:2 * TS2]
    rb = scf[:][:, 2 * TS2:3 * TS2]
    Sb = scf[:][:, 3 * TS2:4 * TS2]
    pc = P.buf("s1_pc", (128, 4 * NCH), F32)
    wsl = [P.buf("s1_wsl0", (128, NCH, 3 * 128), BF16),
           P.buf("s1_wsl1", (128, NCH, 3 * 128), BF16)]
    wstage = P.buf("wstage_s", (128, NCH, 128), F32)

    tdc = pc[:][:, 0:NCH]
    tfc = pc[:][:, NCH:2 * NCH]
    tmc = pc[:][:, 2 * NCH:3 * NCH]
    omt = pc[:][:, 3 * NCH:4 * NCH]
    P.op("sq", lambda e: e.dma_start(out=tdc, in_=w["td"]), dma=True)
    P.op("sq", lambda e: e.dma_start(out=tfc, in_=w["tf"]), dma=True)
    t_p = P.op("sq", lambda e: e.dma_start(out=tmc, in_=w["tm"]), dma=True)
    t_d1 = P.op("se", lambda e: e.activation(tdc, tdc, AF.Exp), after=(t_p,))
    t_d2 = P.op("se", lambda e: e.activation(tdc, tdc, AF.Exp, scale=-1.0), after=(t_d1,))
    t_f = P.op("se", lambda e: e.activation(tfc, tfc, AF.Exp), after=(t_p,))
    t_o = P.op("ve", lambda e: e.tensor_scalar(omt, tmc, -1.0, 1.0, ALU.mult, ALU.add), after=(t_p,))
    decay_c, firstw_c, tm_c, omt_c = tdc, tfc, tmc, omt

    NTS = TS2 // TT

    wstage_cons = [None]

    def load_wsl(name_ap, m, dstbuf, dst_off, after):
        src = name_ap.rearrange("(a p) m -> p a m", p=128)
        td = P.op("sq", lambda e: e.dma_start(out=wstage[:], in_=src[:, :, m * 128:(m + 1) * 128]),
                  after=tuple(after) + (wstage_cons[0],), dma=True)
        tc = P.op("ve", lambda e: e.tensor_copy(dstbuf[:][:, :, dst_off:dst_off + 128], wstage[:]),
                  after=(td,))
        wstage_cons[0] = tc
        return tc

    woslb = [P.buf("s1_wo0", (128, NCH, 128), BF16), P.buf("s1_wo1", (128, NCH, 128), BF16)]
    xtile = [P.buf("s1_xt0", (128, TT), F32), P.buf("s1_xt1", (128, TT), F32)]
    src_wo = w["WoT"].rearrange("(a p) m -> p a m", p=128)
    t_done = []
    t_wo_cons = {0: None, 1: None}
    t_xt_cons = {0: None, 1: None}
    t_rwkv = {}
    t_rwkv_all = []
    prev_ln = (tok0,)
    t_wsl_cons = {0: None, 1: None}
    t_scf_free = None
    for s in range(B_LOC):
        t_xn = None
        for jj in range(NTS):
            j = s * NTS + jj
            t_xn, t_free, _ = emit_ln_tile(
                P, xin_r, j, xn_s[:][:, :, jj * TT:(jj + 1) * TT], ps_a, ps_b,
                after=prev_ln)
            prev_ln = (t_xn, t_free)
        t_xm = []
        for a in range(NCH):
            t1 = P.op("ve", lambda e, a=a: e.tensor_scalar(
                xm_s[:][:, a, :], xn_s[:][:, a, :], tm_c[:, a:a + 1], None, ALU.mult),
                after=(t_xn, t_o))
            t2 = P.op("ve", lambda e, a=a: e.scalar_tensor_tensor(
                xm_s[:][:, a, 1:TS2], xn_s[:][:, a, 0:TS2 - 1], omt_c[:, a:a + 1],
                xm_s[:][:, a, 1:TS2], ALU.mult, ALU.add),
                after=(t1,))
            t_xm.append(t2)
        t_xm = tuple(t_xm)

        for m in range(NCH):
            bi = m % 2
            t_w = None
            for wi, wn in enumerate(("WkT", "WvT", "WrT")):
                t_w = load_wsl(w[wn], m, wsl[bi], wi * 128,
                               (t_wsl_cons[bi],))
            t_ev = {}
            order = [(wi, jj) for wi in range(3) for jj in range(NTS)]
            for q, (wi, jj) in enumerate(order):
                psx = [ps_a, ps_b][q % 2]
                for kk in range(NCH):
                    P.op("te", lambda e, wi=wi, jj=jj, kk=kk, bi=bi, psx=psx: e.matmul(
                        psx[:][:, :TT],
                        wsl[bi][:][:, kk, wi * 128:(wi + 1) * 128],
                        xm_s[:][:, kk, jj * TT:(jj + 1) * TT],
                        start=(kk == 0), stop=(kk == NCH - 1)),
                        after=(t_w,) + t_xm + (t_ev.get(q - 2),),
                        inc=(kk == NCH - 1))
                t_mm = P.token("te")
                dst = (kbuf, kvb, rb)[wi]
                seg = dst[:, jj * TT:(jj + 1) * TT]
                if wi == 0:
                    t_e = P.op("se", lambda e, seg=seg, psx=psx: e.activation(
                        seg, psx[:][:, :TT], AF.Exp), after=(t_mm, t_scf_free))
                    t_e = P.op("ve", lambda e, seg=seg: e.tensor_scalar(
                        seg, seg, 1.142007389e26, None, ALU.min), after=(t_e,))
                elif wi == 2:
                    t_e = P.op("se", lambda e, seg=seg, psx=psx: e.activation(
                        seg, psx[:][:, :TT], AF.Sigmoid), after=(t_mm, t_scf_free))
                else:
                    t_e = P.op("ve", lambda e, seg=seg, psx=psx: e.tensor_copy(
                        seg, psx[:][:, :TT]), after=(t_mm, t_scf_free))
                t_ev[q] = t_e
            t_wsl_cons[bi] = P.token("te")
            t_k = tuple(t_ev[0 * NTS + jj] for jj in range(NTS))
            t_v = tuple(t_ev[1 * NTS + jj] for jj in range(NTS))
            t_r = tuple(t_ev[2 * NTS + jj] for jj in range(NTS))
            t_kv = P.op("ve", lambda e: e.tensor_tensor(kvb, kbuf, kvb, ALU.mult),
                        after=t_k + t_v)
            dbc = decay_c[:, m:m + 1].broadcast_to([128, TS2])
            t_s1_ = P.op("ve", lambda e, dbc=dbc: e.tensor_tensor_scan(Sb, dbc, kvb, 0.0, ALU.mult, ALU.add),
                         after=(t_kv, t_d2))
            t_w1 = P.op("ve", lambda e, m=m: e.tensor_scalar(
                kvb[:, 0:1], kvb[:, 0:1], firstw_c[:, m:m + 1], None, ALU.mult),
                after=(t_s1_, t_f))
            t_w2 = P.op("ve", lambda e, m=m: e.scalar_tensor_tensor(
                kvb[:, 1:TS2], kvb[:, 1:TS2], firstw_c[:, m:m + 1], Sb[:, 0:TS2 - 1],
                ALU.mult, ALU.add), after=(t_w1,))
            t_s2_ = P.op("ve", lambda e, dbc=dbc: e.tensor_tensor_scan(Sb, dbc, kbuf, 0.0, ALU.mult, ALU.add),
                         after=(t_w2, t_kv))
            t_k1 = P.op("ve", lambda e, m=m: e.tensor_scalar(
                kbuf[:, 0:1], kbuf[:, 0:1], firstw_c[:, m:m + 1], None, ALU.mult),
                after=(t_s2_,))
            t_k2 = P.op("ve", lambda e, m=m: e.scalar_tensor_tensor(
                kbuf[:, 1:TS2], kbuf[:, 1:TS2], firstw_c[:, m:m + 1], Sb[:, 0:TS2 - 1],
                ALU.mult, ALU.add), after=(t_k1,))
            t_k3 = P.op("ve", lambda e: e.tensor_scalar(kbuf, kbuf, 1e-16, None, ALU.add), after=(t_k2,))
            t_k4 = P.op("ve", lambda e: e.reciprocal(kbuf, kbuf), after=(t_k3,))
            t_m1 = P.op("ve", lambda e: e.tensor_tensor(kvb, kvb, kbuf, ALU.mult),
                        after=(t_k4, t_w2))
            t_m2 = P.op("ve", lambda e, m=m: e.tensor_tensor(
                rwkv[:][:, m, :], rb, kvb, ALU.mult),
                after=(t_m1,) + t_r)
            t_rwkv[(s, m)] = t_m2
            t_rwkv_all.append(t_m2)
            t_scf_free = t_m2
        t_rwkv_seq = (t_rwkv[(s, 0)],) + tuple(t_rwkv[(s, m)] for m in range(NCH))
        for j2 in range(NCH):
            bi = j2 % 2
            td = P.op("sq", lambda e, j2=j2: e.dma_start(
                out=wstage[:], in_=src_wo[:, :, j2 * 128:(j2 + 1) * 128]),
                after=(t_wo_cons[bi], wstage_cons[0]), dma=True)
            t_w = P.op("ve", lambda e, bi=bi: e.tensor_copy(woslb[bi][:], wstage[:]), after=(td,))
            wstage_cons[0] = t_w
            for jj in range(NTS):
                jt = s * NTS + jj
                psx = [ps_c, ps_d][jj % 2]
                xbi = jj % 2
                t_xl = P.op("sq", lambda e, j2=j2, jt=jt, xbi=xbi: e.dma_start(
                    out=xtile[xbi][:], in_=xin_r[:, j2, jt * TT:(jt + 1) * TT]),
                    after=(t_xt_cons[xbi],), dma=True)
                for m in range(NCH):
                    P.op("te", lambda e, jj=jj, m=m, bi=bi, psx=psx: e.matmul(
                        psx[:][:, :TT], woslb[bi][:][:, m, :],
                        rwkv[:][:, m, jj * TT:(jj + 1) * TT],
                        start=(m == 0), stop=(m == NCH - 1)),
                        after=(t_w, t_xt_cons[xbi]) + t_rwkv_seq,
                        inc=(m == NCH - 1))
                t_mm = P.token("te")
                t_add = P.op("ve", lambda e, xbi=xbi, psx=psx: e.tensor_tensor(
                    xtile[xbi][:], xtile[xbi][:], psx[:][:, :TT], ALU.add),
                    after=(t_mm, t_xl))
                t_st = P.op("po", lambda e, j2=j2, jt=jt, xbi=xbi: e.dma_start(
                    out=xout_r[:, j2, jt * TT:(jt + 1) * TT], in_=xtile[xbi][:]),
                    after=(t_add,), dma=True)
                t_xt_cons[xbi] = t_st
                t_done.append(t_st)
            t_wo_cons[bi] = P.token("te")
        prev_ln = (P.token("te"), P.token("ve"), P.token("se"))
    return tuple(t_done)


def build_s1():
    nc = bass.Bass("TRN2", target_bir_lowering=False)
    xin = nc.dram_tensor("x_in", [D, NT], F32, kind="ExternalInput").ap()
    w = {}
    for n in ("WkT", "WvT", "WrT", "WoT"):
        w[n] = nc.dram_tensor(n, [D, D], F32, kind="ExternalInput").ap()
    for n in ("td", "tf", "tm"):
        w[n] = nc.dram_tensor(n, [128, NCH], F32, kind="ExternalInput").ap()
    xout = nc.dram_tensor("x_out", [D, NT], F32, kind="ExternalOutput").ap()
    P = Prog(nc)
    tok0 = emit_const_init(P)[-1]
    toks = emit_s1(P, xin, xout, w, tok0)
    _finish(P, toks)
    P.realize()
    return nc



def to_col128(v):
    return np.ascontiguousarray(v.reshape(-1, 128).T)


def _finish(P, toks):
    P.wait("sq", toks)


def build_s7_nc():
    nc = bass.Bass("TRN2", target_bir_lowering=False)
    xin = nc.dram_tensor("x_in", [D, NT], F32, kind="ExternalInput").ap()
    WT = nc.dram_tensor("WT", [D, D], F32, kind="ExternalInput").ap()
    b_col = nc.dram_tensor("b_col", [128, NCH], F32, kind="ExternalInput").ap()
    xout = nc.dram_tensor("x_out", [D, NT], F32, kind="ExternalOutput").ap()
    P = Prog(nc)
    tok0 = emit_const_init(P)[-1]
    toks = emit_s7(P, xin, xout, {"WT": WT, "b_col": b_col}, tok0)
    _finish(P, toks)
    P.realize()
    return nc



def _host_l0_tables():
    HD, WINDOW, QK_SCALE = 64, 256, 8.0
    half = np.arange(0, HD, 2, dtype=np.float64)
    inv = 1.0 / 10000.0 ** (half / HD)
    t = np.arange(2048, dtype=np.float64)
    fr = t[:, None] * inv[None, :]
    fr = np.concatenate([fr, fr], -1)
    base = (half + 0.4 * HD) / (1.4 * HD)
    power = (t - 1024) / (WINDOW // 2)
    sc = base[None, :] ** power[:, None]
    sc = np.concatenate([sc, sc], -1)
    cos, sin = np.cos(fr), np.sin(fr)

    def fold(x):
        x = x.copy()
        x[:, :32] = -x[:, :32]
        return x
    arr = lambda x: np.ascontiguousarray(x.reshape(16, 128, 64).transpose(1, 0, 2)).astype(np.float32)
    return (arr(cos * sc * QK_SCALE), arr(fold(sin * sc * QK_SCALE)),
            arr(cos / sc), arr(fold(sin / sc)))


def _host_l0_masks():
    m = np.full((128, 4, 512), -1e30, np.float32)
    kk = np.arange(512)
    for sub in range(2):
        for ql in range(128):
            i = sub * 128 + ql
            m[ql, sub, :] = np.where((kk >= i) & (kk <= i + 256), 0, -1e30)
            m[ql, 2 + sub, :] = np.where((kk >= 256) & (kk - 256 <= i), 0, -1e30)
    return np.ascontiguousarray(m.reshape(128, 4 * 512))


def _host_s3_tables():
    HD, ROPE_BASE = 64, 500000.0
    inv = 1.0 / ROPE_BASE ** (np.arange(0, HD, 2, dtype=np.float64) / HD)
    t = np.arange(2048, dtype=np.float64)
    fr = t[:, None] * inv[None, :]
    emb = np.concatenate([fr, fr], -1)
    cos, sin = np.cos(emb), np.sin(emb)
    sinx = sin.copy()
    sinx[:, :32] = -sin[:, :32]
    arr = lambda x: np.ascontiguousarray(x.reshape(16, 128, 64).transpose(1, 0, 2)).astype(np.float32)
    return arr(cos), arr(sinx)


def _host_s5_masks():
    maskf = np.zeros((128, 4, 512), np.float32)
    maskb = np.zeros((128, 4, 512), np.float32)
    kk = np.arange(512)
    for v_ in range(4):
        for ql in range(128):
            maskf[ql, v_, :] = np.where(kk <= v_ * 128 + ql, 0, -1e30)
            maskb[ql, v_, :] = np.where(kk >= v_ * 128 + ql, 0, -1e30)
    return maskf.reshape(128, -1), maskb.reshape(128, -1)


_BUILD_CACHE = {}


def _get(name, fn):
    if name not in _BUILD_CACHE:
        _BUILD_CACHE[name] = fn()
    return _BUILD_CACHE[name]


def kernel(**inputs):
    import ml_dtypes
    T = lambda a: np.ascontiguousarray(np.asarray(a).T)
    CC = np.ascontiguousarray
    idf = np.eye(128, dtype=np.float32)
    idb = np.eye(128).astype(ml_dtypes.bfloat16)

    x = np.asarray(inputs["L0_attn_a_x"], np.float32)
    xs = [CC(x[2 * c:2 * c + 2].reshape(NT, D).T) for c in range(NCORES)]

    nc1 = _get("s1", build_s1)
    w1 = {
        "WkT": T(inputs["L0_attn_a_Wk"]), "WvT": T(inputs["L0_attn_a_Wv"]),
        "WrT": T(inputs["L0_attn_a_Wr"]), "WoT": T(inputs["L0_attn_a_Wo"]),
        "td": to_col128(np.asarray(inputs["L0_attn_a_time_decay"], np.float32)),
        "tf": to_col128(np.asarray(inputs["L0_attn_a_time_first"], np.float32)),
        "tm": to_col128(np.asarray(inputs["L0_attn_a_time_mix"], np.float32)),
    }
    res = run_bass_kernel_spmd(nc1, [dict(w1, x_in=xs[c]) for c in range(NCORES)],
                               core_ids=list(range(NCORES)))
    xs = [res.results[c]["x_out"] for c in range(NCORES)]

    FDIv = 2730
    q_scale = np.asarray(inputs["L0_ffn_q_scale"], np.float32)
    k_scale = np.asarray(inputs["L0_ffn_k_scale"], np.float32)
    ln_g = np.asarray(inputs["L0_ffn_ln_g"], np.float32)
    ln_b = np.asarray(inputs["L0_ffn_ln_b"], np.float32)
    b1 = np.asarray(inputs["L0_ffn_b1"], np.float32)
    b2 = np.asarray(inputs["L0_ffn_b2"], np.float32)
    qscat = np.zeros((6, 256), np.float32)
    percol = np.zeros((6, 128, 110), np.float32)
    for li in range(6):
        qs, ks = q_scale[li], k_scale[li]
        qscat[li, 0:64] = qs
        qscat[li, 64:128] = np.concatenate([qs[32:], qs[:32]])
        qscat[li, 128:192] = ks
        qscat[li, 192:256] = np.concatenate([ks[32:], ks[:32]])
        percol[li, :, 0:8] = ln_g[li].reshape(8, 128).T
        percol[li, :, 8:16] = ln_b[li].reshape(8, 128).T
        b1a = np.zeros(43 * 128, np.float32); b1a[:FDIv] = b1[li, :FDIv]
        b1g = np.zeros(43 * 128, np.float32); b1g[:FDIv] = b1[li, FDIv:]
        percol[li, :, 16:59] = b1a.reshape(43, 128).T
        percol[li, :, 59:102] = b1g.reshape(43, 128).T
        percol[li, :, 102:110] = b2[li].reshape(8, 128).T
    tqc, tqs, tkc, tks = _get("l0tab", _host_l0_tables)
    w2 = {
        "Wqkv": CC(np.asarray(inputs["L0_ffn_Wqkv"], np.float32).reshape(6 * D, 3 * D)),
        "Wo": CC(np.asarray(inputs["L0_ffn_Wo"], np.float32).reshape(6 * D, D)),
        "W1": CC(np.asarray(inputs["L0_ffn_W1"], np.float32).reshape(6 * D, 2 * FDIv)),
        "W2": CC(np.asarray(inputs["L0_ffn_W2"], np.float32).reshape(6 * FDIv, D)),
        "qscat": qscat, "percol": percol,
        "tqc": tqc, "tqs": tqs, "tkc": tkc, "tks": tks,
        "maskl": _get("l0mask", _host_l0_masks), "ident_bf": idb,
    }
    nc2 = _get("s2", build_s2)
    res = run_bass_kernel_spmd(nc2, [dict(w2, x_in=xs[c]) for c in range(NCORES)],
                               core_ids=list(range(NCORES)))
    xs = [res.results[c]["x_out"] for c in range(NCORES)]

    ropec, ropes = _get("s3tab", _host_s3_tables)
    w3 = {
        "Wqkv": np.asarray(inputs["L0_attn_b_W_qkv"], np.float32),
        "Wout": np.asarray(inputs["L0_attn_b_W_out"], np.float32),
        "ident_bf": idb, "ropec": ropec, "ropes": ropes,
    }
    nc3 = _get("s3", build_s3)
    res = run_bass_kernel_spmd(nc3, [dict(w3, x_in=xs[c]) for c in range(NCORES)],
                               core_ids=list(range(NCORES)))
    xs = [res.results[c]["x_out"] for c in range(NCORES)]

    res = run_bass_kernel_spmd(nc2, [dict(w2, x_in=xs[c]) for c in range(NCORES)],
                               core_ids=list(range(NCORES)))
    xs = [res.results[c]["x_out"] for c in range(NCORES)]

    maskf, maskb = _get("s5mask", _host_s5_masks)
    w5 = {
        "WqT": T(inputs["L1_attn_a_Wq"]), "WkT": T(inputs["L1_attn_a_Wk"]),
        "WvT": T(inputs["L1_attn_a_Wv"]), "WfT": T(inputs["L1_attn_a_Wf"]),
        "WoT2": T(inputs["L1_attn_a_Wo"]),
        "maskf": maskf, "maskb": maskb,
        "bf": np.asarray(inputs["L1_attn_a_bf"], np.float32).reshape(1, 16),
        "ident": idf, "ident_bf": idb,
    }
    nc5 = _get("s5", build_s5)
    res = run_bass_kernel_spmd(nc5, [dict(w5, x_in=xs[c]) for c in range(NCORES)],
                               core_ids=list(range(NCORES)))
    xs = [res.results[c]["x_out"] for c in range(NCORES)]

    w6 = {
        "W1": np.asarray(inputs["L1_ffn_W1"], np.float32),
        "W2": np.asarray(inputs["L1_ffn_W2"], np.float32),
        "Wr": np.asarray(inputs["L1_ffn_Wr"], np.float32),
        "br": np.asarray(inputs["L1_ffn_br"], np.float32).reshape(1, NEXP),
        "b1c": CC(np.asarray(inputs["L1_ffn_b1"], np.float32).reshape(NEXP, 32, 128).transpose(0, 2, 1)),
        "b2": np.asarray(inputs["L1_ffn_b2"], np.float32),
        "ident": idf,
        "emask": np.kron(np.eye(8), np.ones((1, 128))).astype(np.float32),
    }
    nc6 = _get("s6", build_s6)
    res = run_bass_kernel_spmd(nc6, [dict(w6, x_in=xs[c]) for c in range(NCORES)],
                               core_ids=list(range(NCORES)))
    xs = [res.results[c]["x_out"] for c in range(NCORES)]

    w7 = {"WT": T(inputs["L1_attn_b_W"]),
          "b_col": to_col128(np.asarray(inputs["L1_attn_b_b"], np.float32))}
    nc7 = _get("s7", build_s7_nc)
    res = run_bass_kernel_spmd(nc7, [dict(w7, x_in=xs[c]) for c in range(NCORES)],
                               core_ids=list(range(NCORES)))
    xs = [res.results[c]["x_out"] for c in range(NCORES)]

    res = run_bass_kernel_spmd(nc6, [dict(w6, x_in=xs[c]) for c in range(NCORES)],
                               core_ids=list(range(NCORES)))
    xs = [res.results[c]["x_out"] for c in range(NCORES)]

    out = np.stack([np.asarray(xs[c]) for c in range(NCORES)])
    return np.ascontiguousarray(out.transpose(0, 2, 1).reshape(16, 2048, 1024)).astype(np.float32)
